# version 02 (final)
# baseline (speedup 1.0000x reference)
import sys
sys.path.insert(0, '/opt/trn_rl_repo')
import numpy as np
import concourse.bass as bass
import concourse.mybir as mybir
from concourse.bass_utils import run_bass_kernel_spmd
from contextlib import ExitStack

F32 = mybir.dt.float32
BF16 = mybir.dt.bfloat16
AF = mybir.ActivationFunctionType
ALU = mybir.AluOpType
AX = mybir.AxisListType

ENGINES = ['sync', 'vector', 'scalar', 'gpsimd', 'tensor']


class Sem:
    __slots__ = ('h', 'name', 'n')
    def __init__(self, handle, name):
        self.h = handle
        self.name = name
        self.n = 0


class Builder:
    def __init__(self, nc, ctx):
        self.nc = nc
        self.ctx = ctx
        self.ops = []
        self._sem_i = 0
        self.consts = {}
        self._const_i = 0
        self.dram = {}
        self._const_cache = {}

    def sem(self, name=None):
        name = name or f's{self._sem_i}'
        self._sem_i += 1
        h = self.ctx.enter_context(self.nc.semaphore(name))
        return Sem(h, name)

    def sb(self, name, shape, dtype):
        return self.ctx.enter_context(self.nc.sbuf_tensor(name, shape, dtype))

    def psum(self, name, shape, dtype=F32):
        return self.ctx.enter_context(self.nc.psum_tensor(name, shape, dtype))

    def din(self, name, shape, dtype=F32):
        ap = self.nc.dram_tensor(name, list(shape), dtype, kind='ExternalInput').ap()
        self.dram[name] = ap
        return ap

    def dout(self, name, shape, dtype=F32):
        ap = self.nc.dram_tensor(name, list(shape), dtype, kind='ExternalOutput').ap()
        self.dram[name] = ap
        return ap

    def dscratch(self, name, shape, dtype=F32):
        ap = self.nc.dram_tensor(name, list(shape), dtype, kind='Internal').ap()
        self.dram[name] = ap
        return ap

    def const_in(self, arr, name=None):
        if name is None:
            key = (arr.shape, arr.dtype.str, arr.tobytes()[:64], hash(arr.tobytes()))
            if key in self._const_cache:
                return self._const_cache[key]
        arr = np.ascontiguousarray(arr)
        nm = name or f'c{self._const_i}'
        self._const_i += 1
        self.consts[nm] = arr
        dt = F32 if arr.dtype == np.float32 else BF16
        ap = self.din(nm, arr.shape, dt)
        if name is None:
            self._const_cache[key] = ap
        return ap

    def E(self, engine, fn):
        self.ops.append((engine, fn))

    def wait(self, engine, sem, count=None):
        c = sem.n if count is None else count
        if c > 0:
            self.E(engine, lambda e, h=sem.h, c=c: e.wait_ge(h, c))

    def op(self, engine, fn, sig=None, inc=1, dr=True):
        if engine in ('vector', 'scalar', 'gpsimd'):
            self.E(engine, lambda e, f=fn: f(e))
            if sig is None:
                if dr:
                    self.E(engine, lambda e: e.drain())
                return None
            sig.n += inc
            self.E(engine, lambda e, h=sig.h, k=inc: e.drain().then_inc(h, k))
            return sig.n
        if sig is None:
            self.E(engine, fn)
            return None
        sig.n += inc
        self.E(engine, lambda e, f=fn, h=sig.h, k=inc: f(e).then_inc(h, k))
        return sig.n

    def dma(self, engine, out, in_, sig, **kw):
        sig.n += 16
        self.E(engine, lambda e, o=out, i=in_, h=sig.h, kw=kw: e.dma_start(out=o, in_=i, **kw).then_inc(h, 16))
        return sig.n

    def finalize(self):
        block = self.ctx.enter_context(self.nc.Block())
        for name in ENGINES:
            section = getattr(block, name)
            def make(nm):
                def body(eng):
                    for en, f in self.ops:
                        if en == nm:
                            f(eng)
                return body
            section(make(name))

    def run(self, in_maps, core_ids=None, **kw):
        core_ids = core_ids if core_ids is not None else list(range(len(in_maps)))
        for m in in_maps:
            m.update(self.consts)
        return run_bass_kernel_spmd(self.nc, in_maps, core_ids=core_ids, **kw)


def mkbuilder():
    nc = bass.Bass('TRN2', target_bir_lowering=False)
    ctx = ExitStack()
    return Builder(nc, ctx)


class UnionArena:
    def __init__(self, b, name, size):
        self.b = b
        self.nc = b.nc
        self.name = name
        self.size = size
        self.backing = b.sb(name + '_bk', [128, size], mybir.dt.uint8)
        self.base = self.nc.lookup_mloc(self.backing).addr
        self._i = 0

    def tile(self, shape, dtype, offset, name=None):
        per_part = 1
        for s in shape[1:]:
            per_part *= s
        per_part *= mybir.dt.size(dtype)
        assert offset % 32 == 0 and offset + per_part <= self.size, \
            f'arena {self.name}: tile {shape} at {offset} (+{per_part}) > {self.size}'
        nm = name or f'{self.name}_t{self._i}'
        self._i += 1
        return self.nc.alloc_sbuf_tensor_at(nm, list(shape), dtype,
                                            offset=self.base + offset)


def load_fvec(b, engine, sbuf_tile, dram_vec, n_oc, sig, od0=0):
    for oc in range(n_oc):
        b.dma(engine, sbuf_tile[:, oc:oc+1],
              dram_vec[od0+oc*128:od0+(oc+1)*128].rearrange('(p one) -> p one', one=1), sig)


class PsumPool:
    def __init__(self, b, n=6, name='ps', dtype=F32, width=512):
        self.b = b
        self.banks = [b.psum(f'{name}{i}', [128, width], dtype) for i in range(n)]
        self.free = [(None, 0)] * n
        self.i = 0
        self.n = n

    def acquire(self):
        idx = self.i % self.n
        self.i += 1
        sem, cnt = self.free[idx]
        if sem is not None:
            self.b.wait('tensor', sem, cnt)
        return idx, self.banks[idx]

    def set_free(self, idx, sem, count):
        self.free[idx] = (sem, count)


class Shared:
    def __init__(self, b, init_sig):
        self.b = b
        self.init_sig = init_sig
        self._consts = {}
        self._ci = 0
        self.ones_col_bf = b.sb('sh_ones_col_bf', [128, 1], BF16)
        self.ones_col_f32 = b.sb('sh_ones_col_f32', [128, 1], F32)
        self.ones_row_f32 = b.sb('sh_ones_row_f32', [1, 128], F32)
        self.ones_row_bf = b.sb('sh_ones_row_bf', [1, 128], BF16)
        b.op('vector', lambda v: v.memset(self.ones_col_bf[:], 1.0))
        b.op('vector', lambda v: v.memset(self.ones_col_f32[:], 1.0))
        b.op('vector', lambda v: v.memset(self.ones_row_f32[:], 1.0))
        b.op('vector', lambda v: v.memset(self.ones_row_bf[:], 1.0), sig=init_sig)

    def const(self, value):
        v = float(value)
        if v in self._consts:
            return self._consts[v]
        t = self.b.sb(f'sh_const{self._ci}', [128, 1], F32)
        self._ci += 1
        self.b.op('vector', lambda ve, t=t, v=v: ve.memset(t[:], v), sig=self.init_sig)
        self._consts[v] = t
        return t

    def make_ident_sel(self):
        if hasattr(self, 'ident_f32'):
            return
        b = self.b
        idn = b.const_in(np.eye(128, dtype=np.float32), 'sh_idn')
        sel = np.zeros((8, 8, 128), np.float32)
        for e in range(8):
            sel[e, e, :] = 1.0
        self.sel8_d = b.const_in(sel, 'sh_sel8')
        self.ident_f32 = b.sb('sh_ident', [128, 128], F32)
        b.dma('sync', self.ident_f32[:], idn[:, :], self.init_sig)
        import ml_dtypes
        idnb = b.const_in(np.eye(128, dtype=np.float32).astype(ml_dtypes.bfloat16), 'sh_idnb')
        self.ident_bf = b.sb('sh_identb', [128, 128], BF16)
        b.dma('sync', self.ident_bf[:], idnb[:, :], self.init_sig)
        s16 = np.zeros((16, 16, 128), np.float32)
        for i in range(16):
            s16[i, i, :] = 1.0
        s16d = b.const_in(s16.astype(ml_dtypes.bfloat16), 'sh_sel16')
        self.sel16_bf = b.sb('sh_sel16b', [16, 16, 128], BF16)
        b.dma('sync', self.sel16_bf[:], s16d[:, :, :], self.init_sig)


def emit_ln(b, sh, pool, x_fn, out_fn, FT, T, scr, mm_sig, ev_sig,
            gamma=None, beta=None, eps=1e-5, in_bf16=False, dbgcb=None):
    D = FT * 128
    tmp = scr['tmp']
    sh.const(eps)
    for ci, t0 in enumerate(range(0, T, 512)):
        t1 = min(t0 + 512, T)
        n = t1 - t0
        par = ci % 2
        mean = scr['mean'][par] if isinstance(scr['mean'], list) else scr['mean']
        var = scr['var'][par] if isinstance(scr['var'], list) else scr['var']
        t2 = scr['t2'][par] if isinstance(scr['t2'], list) else scr['t2']
        xb2 = scr['xb2'][par]
        sq2 = scr['sq2'][par]
        prev = getattr(scr, '_ln_prev', None)
        for ft in range(FT):
            b.op('scalar', lambda s, ft=ft, t0=t0, t1=t1, sq2=sq2, n=n: s.activation(
                out=sq2[:, ft, 0:n], in_=x_fn(ft)[:, t0:t1], func=AF.Square))
            if not in_bf16:
                b.op('vector', lambda v, ft=ft, t0=t0, t1=t1, xb2=xb2, n=n: v.tensor_copy(
                    xb2[:, ft, 0:n], x_fn(ft)[:, t0:t1]))
        b.op('scalar', lambda s: s.nop(), sig=ev_sig)
        if not in_bf16:
            b.op('vector', lambda v: v.engine_nop(), sig=ev_sig)
        prep_done = ev_sig.n
        b.wait('tensor', ev_sig, prep_done)
        ia, bank_a = pool.acquire()
        ib, bank_b = pool.acquire()
        for ft in range(FT):
            if in_bf16:
                b.op('tensor', lambda t, ft=ft, bank=bank_a, t0=t0, t1=t1, n=n: t.matmul(
                    bank[0:1, 0:n], sh.ones_col_bf[:, :], x_fn(ft)[:, t0:t1],
                    start=(ft == 0), stop=(ft == FT - 1)))
            else:
                b.op('tensor', lambda t, ft=ft, bank=bank_a, xb2=xb2, n=n: t.matmul(
                    bank[0:1, 0:n], sh.ones_col_bf[:, :], xb2[:, ft, 0:n],
                    start=(ft == 0), stop=(ft == FT - 1)))
        for ft in range(FT):
            last = ft == FT - 1
            b.op('tensor', lambda t, ft=ft, bank=bank_b, sq2=sq2, n=n: t.matmul(
                bank[0:1, 0:n], sh.ones_col_bf[:, :], sq2[:, ft, 0:n],
                start=(ft == 0), stop=(ft == FT - 1)),
                sig=(mm_sig if last else None))
        mmc = mm_sig.n
        b.wait('vector', mm_sig, mmc)
        b.op('vector', lambda v, bank=bank_a, t0=t0, t1=t1, n=n, mean=mean: v.tensor_scalar(
            mean[0:1, 0:n], bank[0:1, 0:n], 1.0 / D, None, op0=ALU.mult))
        b.op('vector', lambda v, t0=t0, t1=t1, mean=mean, t2=t2: v.scalar_tensor_tensor(
            t2[0:1, 0:n], mean[0:1, 0:n], -1.0, mean[0:1, 0:n],
            op0=ALU.mult, op1=ALU.mult))
        b.op('vector', lambda v, t0=t0, t1=t1, t2=t2: v.tensor_scalar(
            t2[0:1, 0:n], t2[0:1, 0:n], float(eps), None, op0=ALU.add))
        b.op('vector', lambda v, bank=bank_b, t0=t0, t1=t1, n=n, var=var, t2=t2: v.scalar_tensor_tensor(
            var[0:1, 0:n], bank[0:1, 0:n], 1.0 / D, t2[0:1, 0:n],
            op0=ALU.mult, op1=ALU.add))
        b.op('vector', lambda v: v.engine_nop(), sig=ev_sig)
        pool.set_free(ia, ev_sig, ev_sig.n)
        pool.set_free(ib, ev_sig, ev_sig.n)
        var_done = ev_sig.n
        b.wait('scalar', ev_sig, var_done)
        b.op('scalar', lambda s, t0=t0, t1=t1, var=var, t2=t2: s.activation(
            out=t2[0:1, 0:n], in_=var[0:1, 0:n], func=AF.Sqrt), sig=ev_sig)
        b.wait('vector', ev_sig, ev_sig.n)
        b.op('vector', lambda v, t0=t0, t1=t1, var=var, t2=t2: v.reciprocal(
            var[0:1, 0:n], t2[0:1, 0:n]), sig=ev_sig)
        rstd_done = ev_sig.n
        b.wait('tensor', ev_sig, rstd_done)
        im, bank_m = pool.acquire()
        ir, bank_r = pool.acquire()
        b.op('tensor', lambda t, bank=bank_m, t0=t0, t1=t1, n=n, mean=mean: t.matmul(
            bank[:, 0:n], sh.ones_row_f32[:, :], mean[0:1, 0:n], start=True, stop=True))
        b.op('tensor', lambda t, bank=bank_r, t0=t0, t1=t1, n=n, var=var: t.matmul(
            bank[:, 0:n], sh.ones_row_f32[:, :], var[0:1, 0:n], start=True, stop=True),
            sig=mm_sig)
        bc_done = mm_sig.n
        b.wait('vector', mm_sig, bc_done)
        if dbgcb is not None:
            dbgcb(ci, bank_m, bank_r)
        for ft in range(FT):
            b.op('vector', lambda v, ft=ft, bank=bank_m, t0=t0, t1=t1, n=n:
                 v.tensor_tensor(tmp[:, 0:n], x_fn(ft)[:, t0:t1], bank[:, 0:n], op=ALU.subtract))
            if gamma is None:
                b.op('vector', lambda v, ft=ft, bank=bank_r, t0=t0, t1=t1, n=n:
                     v.tensor_tensor(out_fn(ft, t0, t1), tmp[:, 0:n], bank[:, 0:n], op=ALU.mult))
            else:
                b.op('vector', lambda v, ft=ft, bank=bank_r, t0=t0, t1=t1, n=n:
                     v.tensor_tensor(tmp[:, 0:n], tmp[:, 0:n], bank[:, 0:n], op=ALU.mult))
                b.op('vector', lambda v, ft=ft, t0=t0, t1=t1, n=n:
                     v.tensor_scalar(out_fn(ft, t0, t1), tmp[:, 0:n],
                                     gamma[:, ft:ft+1], beta[:, ft:ft+1],
                                     op0=ALU.mult, op1=ALU.add))
        b.op('vector', lambda v: v.engine_nop(), sig=ev_sig)
        pool.set_free(im, ev_sig, ev_sig.n)
        pool.set_free(ir, ev_sig, ev_sig.n)
    return ev_sig.n


def ln_scratch(b, T):
    return {
        'mean': b.sb('ln_mean', [1, T], F32),
        'var': b.sb('ln_var', [1, T], F32),
        't2': b.sb('ln_t2', [1, T], F32),
        'tmp': b.sb('ln_tmp', [128, 512], F32),
        'xb2': [b.sb(f'ln_xb2_{i}', [128, 8, 512], BF16) for i in range(2)],
        'sq2': [b.sb(f'ln_sq2_{i}', [128, 8, 512], BF16) for i in range(2)],
    }


def gemm(b, pool, mm_sig, wsb, act_fn, KT, OD, T, evac, tchunk=512,
         od0=0, w_ready=None):
    if w_ready is not None:
        b.wait('tensor', w_ready[0], w_ready[1])
    n_oc = (OD + 127) // 128
    for t0 in range(0, T, tchunk):
        t1 = min(t0 + tchunk, T)
        for oc in range(n_oc):
            m0 = oc * 128
            m1 = min(m0 + 128, OD)
            idx, bank = pool.acquire()
            for kt in range(KT):
                st, sp = (kt == 0), (kt == KT - 1)
                b.op('tensor', lambda t, kt=kt, bank=bank, m0=m0, m1=m1, t0=t0, t1=t1, st=st, sp=sp:
                     t.matmul(bank[0:m1-m0, 0:t1-t0], wsb[:, kt, od0+m0:od0+m1],
                              act_fn(kt, t0, t1), start=st, stop=sp),
                     sig=(mm_sig if sp else None))
            sem, cnt = evac(b, bank, oc, t0, t1, mm_sig.n)
            pool.set_free(idx, sem, cnt)


class WStream:
    def __init__(self, b, name, shape, dtype, ld_engine='gpsimd', nbuf=2):
        self.b = b
        self.bufs = [b.sb(f'{name}_b{i}', list(shape), dtype) for i in range(nbuf)]
        self.sig = b.sem(f'{name}_ld')
        self.engine = ld_engine
        self.free = [(None, 0)] * nbuf
        self.i = 0
        self.nbuf = nbuf

    def load(self, dma_list):
        k = self.i % self.nbuf
        self.i += 1
        sem, cnt = self.free[k]
        if sem is not None:
            self.b.wait(self.engine, sem, cnt)
        buf = self.bufs[k]
        for dst_fn, src in dma_list:
            self.b.dma(self.engine, dst_fn(buf), src, self.sig)
        return k, buf, self.sig.n

    def release(self, k, sem, cnt):
        self.free[k] = (sem, cnt)


def f0_params(b, pfx='L0_ffn_'):
    d = {}
    d['ln1_w'] = b.din(pfx+'ln1_w', [1024], F32)
    d['ln1_b'] = b.din(pfx+'ln1_b', [1024], F32)
    d['ln2_w'] = b.din(pfx+'ln2_w', [1024], F32)
    d['ln2_b'] = b.din(pfx+'ln2_b', [1024], F32)
    d['w_qkv'] = b.din(pfx+'w_qkv', [1024, 3072], F32)
    d['b_qkv'] = b.din(pfx+'b_qkv', [3072], F32)
    d['w_o'] = b.din(pfx+'w_o', [1024, 1024], F32)
    d['b_o'] = b.din(pfx+'b_o', [1024], F32)
    d['w_gate'] = b.din(pfx+'w_gate', [1024, 8], F32)
    d['w1'] = b.din(pfx+'w1', [8, 1024, 4096], F32)
    d['b1'] = b.din(pfx+'b1', [8, 4096], F32)
    d['w2'] = b.din(pfx+'w2', [8, 4096, 1024], F32)
    d['b2'] = b.din(pfx+'b2', [8, 1024], F32)
    return d


def barrier(b, engines, sems):
    for e in engines:
        for s in sems:
            b.wait(e, s)


def f0_dram(b, pfx):
    d = {}
    d['fv128'] = b.din(pfx+'fv128', [128, 48], F32)
    d['fv64'] = b.din(pfx+'fv64', [64, 32], F32)
    d['bvrow'] = b.din(pfx+'bvrow', [1, 1024], F32)
    d['b2m'] = b.din(pfx+'b2m', [8, 1024], F32)
    d['wgate'] = b.din(pfx+'wgate', [128, 64], F32)
    d['wqkvh'] = b.din(pfx+'wqkvh', [128, 24576], F32)
    d['woh'] = b.din(pfx+'woh', [128, 8192], F32)
    d['w1h'] = b.din(pfx+'w1h', [8, 128, 32768], F32)
    d['w2h'] = b.din(pfx+'w2h', [8, 128, 32768], F32)
    d['b1v'] = b.din(pfx+'b1v', [128, 256], F32)
    return d


def f0_host_prep(I, pfx_in='L0_ffn_'):
    g = lambda k: np.asarray(I[pfx_in + k], dtype=np.float32)
    fvec = lambda v: np.ascontiguousarray(v.reshape(-1, 128).T)
    out = {}
    fv128 = np.zeros((128, 48), np.float32)
    fv128[:, 0:8] = fvec(g('ln1_w')); fv128[:, 8:16] = fvec(g('ln1_b'))
    fv128[:, 16:24] = fvec(g('ln2_w')); fv128[:, 24:32] = fvec(g('ln2_b'))
    fv128[:, 32:40] = fvec(g('b_o'))
    out['fv128'] = fv128
    b_qkv = g('b_qkv')
    fv64 = np.zeros((64, 32), np.float32)
    fv64[:, 0:16] = b_qkv[0:1024].reshape(16, 64).T
    fv64[:, 16:32] = b_qkv[1024:2048].reshape(16, 64).T
    out['fv64'] = fv64
    out['bvrow'] = np.ascontiguousarray(b_qkv[2048:3072].reshape(1, 1024))
    out['b2m'] = np.ascontiguousarray(g('b2'))
    out['wgate'] = np.ascontiguousarray(g('w_gate').reshape(8, 128, 8).transpose(1, 0, 2).reshape(128, 64))
    lhsT = lambda w: np.ascontiguousarray(
        w.reshape(w.shape[0] // 128, 128, w.shape[1]).transpose(1, 0, 2).reshape(128, -1))
    out['wqkvh'] = lhsT(g('w_qkv'))
    out['woh'] = lhsT(g('w_o'))
    out['w1h'] = np.stack([lhsT(g('w1')[e]) for e in range(8)])
    out['w2h'] = np.stack([lhsT(g('w2')[e]) for e in range(8)])
    out['b1v'] = np.ascontiguousarray(g('b1').reshape(8 * 32, 128).T)
    return out


def emit_f0(b, sh, pool, W, x, ar, mm, ev, ldw, T=2048, coef=0.5, tag='f0', phases=6):
    FT = 8
    NH = 16
    lam = 1.0 / 8.0
    EB = -10.0
    K = lambda n: n * 1024
    ALL = ENGINES

    sh.make_ident_sel()
    sh.const(EB)
    sh.const(1e-5)
    barrier(b, ALL, [sh.init_sig])

    u = ar.tile([128, FT, T], BF16, K(0), f'{tag}_u')
    t = ar.tile([128, FT, T], BF16, K(32), f'{tag}_t')
    ctx = ar.tile([128, FT, T], BF16, K(64), f'{tag}_ctx')
    h = ar.tile([128, FT, T], BF16, K(96), f'{tag}_h')

    fv128 = b.sb(f'{tag}s_fv128', [128, 48], F32)
    fv64 = b.sb(f'{tag}s_fv64', [64, 32], F32)
    bvrow = b.sb(f'{tag}s_bvrow', [1, 1024], BF16)
    b2bf = b.sb(f'{tag}s_b2bf', [8, 1024], BF16)
    wgate = b.sb(f'{tag}s_wgate', [128, 8, 8], BF16)
    b1v = b.sb(f'{tag}s_b1v', [128, 256], F32)
    b.dma('sync', fv128[:], W['fv128'][:, :], ldw)
    b.dma('sync', fv64[:], W['fv64'][:, :], ldw)
    b.dma('gpsimd', bvrow[:], W['bvrow'][:, :], ldw)
    b.dma('gpsimd', b2bf[:], W['b2m'][:, :], ldw)
    b.dma('gpsimd', wgate[:], W['wgate'][:, :].rearrange('p (a c) -> p a c', a=8), ldw)
    b.dma('sync', b1v[:], W['b1v'][:, :], ldw)
    barrier(b, ALL, [ldw])

    def lnscr(rows_off, cb_off, with_xb):
        d = {}
        d['mean'] = ar.tile([1, 512], F32, rows_off)
        d['var'] = ar.tile([1, 512], F32, rows_off + K(2))
        d['t2'] = ar.tile([1, 512], F32, rows_off + K(4))
        d['tmp'] = ar.tile([128, 512], F32, rows_off + K(6))
        d['sq2'] = [ar.tile([128, FT, 512], BF16, cb_off + i * K(8)) for i in range(2)]
        d['xb2'] = ([ar.tile([128, FT, 512], BF16, cb_off + K(16) + i * K(8)) for i in range(2)]
                    if with_xb else [None, None])
        return d

    if phases == 60:
        t_in = b.din(f'{tag}_t_in', [1024, T], F32)
        for ft in range(FT):
            b.dma('gpsimd', t[:, ft, :], t_in[ft*128:(ft+1)*128, :], ldw)
        barrier(b, ALL, [ldw])
        _moe_only = True
    else:
        _moe_only = False
    if not _moe_only:
        emit_ln(b, sh, pool, lambda ft: x[:, ft, :], lambda ft, t0, t1: u[:, ft, t0:t1],
                FT, T, lnscr(K(32), K(64), True), mm, ev)
        barrier(b, ALL, [ev, mm])
    if not _moe_only:
        if phases < 2:
            return
        emit_ln(b, sh, pool, lambda ft: u[:, ft, :], lambda ft, t0, t1: t[:, ft, t0:t1],
                FT, T, lnscr(K(64), K(72), False), mm, ev,
                gamma=fv128[:, 0:8], beta=fv128[:, 8:16], in_bf16=True)
        barrier(b, ALL, [ev, mm])
        if phases < 3:
            return

    if not _moe_only:
        qh = ar.tile([64, T], BF16, K(96), f'{tag}_qh')
        kh = ar.tile([64, T], BF16, K(100), f'{tag}_kh')
        vtm = ar.tile([128, 16, 256], BF16, K(104), f'{tag}_vtm')
        eT = ar.tile([128, 8, 512], BF16, K(112), f'{tag}_eT')
        wv = [ar.tile([128, 8, 256], BF16, K(120), f'{tag}_wv{i}') for i in range(2)]
        zrow = ar.tile([1, 512], F32, K(124), f'{tag}_zrow')
        zinv = ar.tile([64, 512], F32, K(126), f'{tag}_zinv')
        wq = [ar.tile([128, 8, 64], BF16, K(128) + i * 1024, f'{tag}_wq{i}') for i in range(2)]
        wk = [ar.tile([128, 8, 64], BF16, K(130) + i * 1024, f'{tag}_wk{i}') for i in range(2)]
        wq_free = [(None, 0)] * 2
        wk_free = [(None, 0)] * 2
        wv_free = [(None, 0)] * 2
        zps = b.psum(f'{tag}_zps', [128, 512], F32)
        cps = b.psum(f'{tag}_cps', [128, 512], F32)
        zfree = (None, 0)
        cfree = (None, 0)
        qkvr = W['wqkvh'][:, :].rearrange('p (a c) -> p a c', c=3072)

        for hd in range(NH):
            par = hd % 2
            g4 = hd // 4
            for (wfree, wbuf, c0_) in ((wq_free, wq[par], hd*64), (wk_free, wk[par], 1024+hd*64)):
                s0, c0 = wfree[par]
                if s0 is not None:
                    b.wait('gpsimd', s0, c0)
                b.dma('gpsimd', wbuf[:], qkvr[:, :, c0_:c0_+64], ldw)
            if hd % 4 == 0:
                vpar = g4 % 2
                s0, c0 = wv_free[vpar]
                if s0 is not None:
                    b.wait('gpsimd', s0, c0)
                b.dma('gpsimd', wv[vpar][:], qkvr[:, :, 2048+g4*256:2048+(g4+1)*256], ldw)
            b.wait('tensor', ldw, ldw.n)

            for wbuf, dst, bcol in ((wq[par], qh, hd), (wk[par], kh, 16 + hd)):
                for tc in range(4):
                    idx, bank = pool.acquire()
                    for kt in range(FT):
                        b.op('tensor', lambda te, kt=kt, bank=bank, wbuf=wbuf, tc=tc: te.matmul(
                            bank[0:64, 0:512], wbuf[:, kt, :], t[:, kt, tc*512:(tc+1)*512],
                            start=(kt == 0), stop=(kt == FT - 1)),
                            sig=(mm if kt == FT - 1 else None))
                    b.wait('scalar', mm, mm.n)
                    b.op('scalar', lambda s, bank=bank, dst=dst, tc=tc, bcol=bcol: s.activation(
                        out=dst[:, tc*512:(tc+1)*512], in_=bank[0:64, 0:512], func=AF.Identity,
                        bias=fv64[:, bcol:bcol+1]), sig=ev)
                    pool.set_free(idx, ev, ev.n)
            wq_free[par] = (ev, ev.n)
            wk_free[par] = (ev, ev.n)

            if hd % 4 == 0:
                for tc16 in range(16):
                    idx, bank = pool.acquire()
                    for kt in range(FT):
                        b.op('tensor', lambda te, kt=kt, bank=bank, tc16=tc16, vpar=vpar: te.matmul(
                            bank[:, 0:256], t[:, kt, tc16*128:(tc16+1)*128], wv[vpar][:, kt, :],
                            start=(kt == 0), stop=False))
                    b.op('tensor', lambda te, bank=bank, g4=g4: te.matmul(
                        bank[:, 0:256], sh.ones_row_bf[:, :], bvrow[0:1, g4*256:(g4+1)*256],
                        start=False, stop=True), sig=mm)
                    b.wait('vector', mm, mm.n)
                    b.op('vector', lambda v, bank=bank, tc16=tc16: v.tensor_copy(
                        vtm[:, tc16, :], bank[:, 0:256]), sig=ev)
                    pool.set_free(idx, ev, ev.n)
                wv_free[vpar] = (ev, ev.n)
            vslice = (hd % 4) * 64
            b.wait('tensor', ev, ev.n)

            for qc in range(4):
                bz, bc = zps, cps
                if zfree[0] is not None:
                    b.wait('tensor', zfree[0], zfree[1])
                if cfree[0] is not None:
                    b.wait('tensor', cfree[0], cfree[1])
                for blk in range(2):
                    b.wait('scalar', mm, mm.n)
                    for j in range(8):
                        kc = blk * 8 + j
                        idx, bank = pool.acquire()
                        b.op('tensor', lambda te, bank=bank, kc=kc, qc=qc: te.matmul(
                            bank[:, 0:512], kh[:, kc*128:(kc+1)*128], qh[:, qc*512:(qc+1)*512],
                            start=True, stop=True), sig=mm)
                        b.wait('scalar', mm, mm.n)
                        b.op('scalar', lambda s, bank=bank, j=j: s.activation(
                            out=eT[:, j, :], in_=bank[:, 0:512], func=AF.Exp,
                            bias=sh.const(EB)[:, 0:1], scale=lam), sig=ev)
                        pool.set_free(idx, ev, ev.n)
                    b.wait('tensor', ev, ev.n)
                    for j in range(8):
                        kc = blk * 8 + j
                        b.op('tensor', lambda te, bz=bz, j=j, kc=kc: te.matmul(
                            bz[0:1, 0:512], sh.ones_col_bf[:, :], eT[:, j, :],
                            start=(kc == 0), stop=(kc == 15)),
                            sig=(mm if j == 7 else None))
                        b.op('tensor', lambda te, bc=bc, j=j, kc=kc, vslice=vslice: te.matmul(
                            bc[0:64, 0:512], vtm[:, kc, vslice:vslice+64], eT[:, j, :],
                            start=(kc == 0), stop=(kc == 15)),
                            sig=(mm if j == 7 else None))
                b.wait('vector', mm, mm.n)
                b.op('vector', lambda v, bz=bz: v.reciprocal(zrow[0:1, :], bz[0:1, 0:512]), sig=ev)
                zfree = (ev, ev.n)
                b.wait('tensor', ev, ev.n)
                ib, bb = pool.acquire()
                b.op('tensor', lambda te, bb=bb: te.matmul(
                    bb[0:64, 0:512], sh.ones_row_f32[0:1, 0:64], zrow[0:1, :],
                    start=True, stop=True), sig=mm)
                b.wait('vector', mm, mm.n)
                b.op('vector', lambda v, bb=bb: v.tensor_copy(zinv[:, :], bb[0:64, 0:512]), sig=ev)
                pool.set_free(ib, ev, ev.n)
                b.wait('vector', mm, mm.n)
                b.wait('vector', ev, ev.n)
                ftc, pbase = hd // 2, (hd % 2) * 64
                b.op('vector', lambda v, bc=bc, ftc=ftc, pbase=pbase, qc=qc: v.tensor_tensor(
                    ctx[pbase:pbase+64, ftc, qc*512:(qc+1)*512], bc[0:64, 0:512], zinv[:, :],
                    op=ALU.mult), sig=ev)
                cfree = (ev, ev.n)

        barrier(b, ALL, [ev, mm])
        if phases < 4:
            return
        wo = ar.tile([128, 8, 1024], BF16, K(32), f'{tag}_wo')
        b.dma('gpsimd', wo[:], W['woh'][:, :].rearrange('p (a c) -> p a c', a=8), ldw)
        for ft in range(FT):
            b.op('vector', lambda v, ft=ft: v.tensor_scalar(
                h[:, ft, :], u[:, ft, :], fv128[:, 32+ft:33+ft], None, op0=ALU.add), sig=ev)
        b.wait('tensor', ldw, ldw.n)
        b.wait('tensor', ev, ev.n)
        for tc in range(4):
            for oc in range(FT):
                idx, bank = pool.acquire()
                for kt in range(FT):
                    b.op('tensor', lambda te, kt=kt, bank=bank, oc=oc, tc=tc: te.matmul(
                        bank[:, 0:512], wo[:, kt, oc*128:(oc+1)*128], ctx[:, kt, tc*512:(tc+1)*512],
                        start=(kt == 0), stop=(kt == FT - 1)), sig=(mm if kt == FT - 1 else None))
                b.wait('vector', mm, mm.n)
                b.op('vector', lambda v, bank=bank, oc=oc, tc=tc: v.tensor_tensor(
                    h[:, oc, tc*512:(tc+1)*512], h[:, oc, tc*512:(tc+1)*512], bank[:, 0:512],
                    op=ALU.add), sig=ev)
                pool.set_free(idx, ev, ev.n)
        b.wait('vector', ev, ev.n)
        for ft in range(FT):
            b.op('vector', lambda v, ft=ft: v.scalar_tensor_tensor(
                x[:, ft, :], h[:, ft, :], coef, x[:, ft, :], op0=ALU.mult, op1=ALU.add), sig=ev)
        barrier(b, ALL, [ev, mm])
        if phases < 5:
            return
        emit_ln(b, sh, pool, lambda ft: h[:, ft, :], lambda ft, t0, t1: t[:, ft, t0:t1],
                FT, T, lnscr(K(0), K(8), False), mm, ev,
                gamma=fv128[:, 16:24], beta=fv128[:, 24:32], in_bf16=True)
        barrier(b, ALL, [ev, mm])
        if phases < 6:
            return


    sel8t = ar.tile([8, 8, 128], F32, K(126), f'{tag}_sel8t')
    mtmp = ar.tile([128, 512], F32, K(130), f'{tag}_mtmp')
    b.dma('sync', sel8t[:], sh.sel8_d[:, :, :], ldw)
    sel8 = [sel8t[:, e, :] for e in range(8)]
    barrier(b, ['tensor'], [ldw])
    h1 = ar.tile([128, 32, 512], BF16, K(0), f'{tag}_h1')
    w1c = [ar.tile([128, 8, 1024], BF16, K(64 + 16 * i), f'{tag}_w1c{i}') for i in range(2)]
    w2c = [ar.tile([128, 32, 128], BF16, K(96 + 8 * i), f'{tag}_w2c{i}') for i in range(2)]
    weT = ar.tile([8, T], F32, K(112), f'{tag}_weT')
    weTb = ar.tile([8, T], BF16, K(120), f'{tag}_weTb')
    werep = ar.tile([128, 512], F32, K(124), f'{tag}_werep')
    logits = b.sb(f'{tag}s_logits', [128, 16, 8], F32)
    top8 = b.sb(f'{tag}s_top8', [128, 8], F32)
    s12 = b.sb(f'{tag}s_s12', [128, 1], F32)
    nmax = b.sb(f'{tag}s_nmax', [128, 1], F32)

    for tc16 in range(16):
        idx, bank = pool.acquire()
        for kt in range(FT):
            b.op('tensor', lambda te, kt=kt, bank=bank, tc16=tc16: te.matmul(
                bank[:, 0:8], t[:, kt, tc16*128:(tc16+1)*128], wgate[:, kt, :],
                start=(kt == 0), stop=(kt == FT - 1)), sig=(mm if kt == FT - 1 else None))
        b.wait('vector', mm, mm.n)
        b.op('vector', lambda v, bank=bank: v.reduce_max(
            nmax[:, :], bank[:, 0:8], axis=AX.X, negate=True), sig=ev)
        b.wait('scalar', ev, ev.n)
        b.op('scalar', lambda s, bank=bank, tc16=tc16: s.activation(
            out=logits[:, tc16, :], in_=bank[:, 0:8], func=AF.Exp,
            bias=nmax[:, 0:1]), sig=ev)
        pool.set_free(idx, ev, ev.n)
        b.wait('vector', ev, ev.n)
        b.op('vector', lambda v, tc16=tc16: v.max(out=top8[:, :], in_=logits[:, tc16, :]))
        b.op('vector', lambda v: v.tensor_tensor(
            s12[:, :], top8[:, 0:1], top8[:, 1:2], op=ALU.add))
        b.op('vector', lambda v: v.reciprocal(s12[:, :], s12[:, :]))
        b.op('vector', lambda v, tc16=tc16: v.scalar_tensor_tensor(
            logits[:, tc16, :], logits[:, tc16, :], top8[:, 1:2], logits[:, tc16, :],
            op0=ALU.is_ge, op1=ALU.mult))
        b.op('vector', lambda v, tc16=tc16: v.tensor_scalar(
            logits[:, tc16, :], logits[:, tc16, :], s12[:, 0:1], 0.5,
            op0=ALU.mult, op1=ALU.mult), sig=ev)
    b.wait('tensor', ev, ev.n)
    for tc16 in range(16):
        idx, bank = pool.acquire()
        b.op('tensor', lambda te, bank=bank, tc16=tc16: te.matmul(
            bank[0:8, 0:128], logits[:, tc16, :], sh.ident_f32[:, :],
            start=True, stop=True), sig=mm)
        b.wait('vector', mm, mm.n)
        b.op('vector', lambda v, bank=bank, tc16=tc16: v.tensor_copy(
            weT[:, tc16*128:(tc16+1)*128], bank[0:8, 0:128]), sig=ev)
        pool.set_free(idx, ev, ev.n)
    b.op('vector', lambda v: v.tensor_copy(weTb[:], weT[:]), sig=ev)
    weT_done = ev.n

    w1_free = [(None, 0)] * 2
    w2_free = [(None, 0)] * 2
    wi = w2i = 0
    for e in range(8):
        w1r = W['w1h'][e].rearrange('p (a c) -> p a c', a=8)
        w2r = W['w2h'][e].rearrange('p (a c) -> p a c', a=32)
        for tc in range(4):
            b.wait('tensor', ev, weT_done)
            ir, br = pool.acquire()
            b.op('tensor', lambda te, br=br, e=e, tc=tc: te.matmul(
                br[:, 0:512], sel8[e], weT[:, tc*512:(tc+1)*512],
                start=True, stop=True), sig=mm)
            b.wait('vector', mm, mm.n)
            b.op('vector', lambda v, br=br: v.tensor_copy(werep[:, :], br[:, 0:512]), sig=ev)
            pool.set_free(ir, ev, ev.n)
            werep_done = ev.n
            for hc in range(4):
                par = wi % 2
                wi += 1
                s0, c0 = w1_free[par]
                if s0 is not None:
                    b.wait('gpsimd', s0, c0)
                b.dma('gpsimd', w1c[par][:], w1r[:, :, hc*1024:(hc+1)*1024], ldw)
                b.wait('tensor', ldw, ldw.n)
                for oc in range(8):
                    idx, bank = pool.acquire()
                    for kt in range(FT):
                        b.op('tensor', lambda te, kt=kt, bank=bank, par=par, oc=oc, tc=tc: te.matmul(
                            bank[:, 0:512], w1c[par][:, kt, oc*128:(oc+1)*128],
                            t[:, kt, tc*512:(tc+1)*512],
                            start=(kt == 0), stop=(kt == FT - 1)), sig=(mm if kt == FT - 1 else None))
                    b.wait('scalar', mm, mm.n)
                    b.op('scalar', lambda s, bank=bank, hc=hc, oc=oc, e=e: s.activation(
                        out=h1[:, hc*8+oc, :], in_=bank[:, 0:512], func=AF.Gelu_apprx_tanh,
                        bias=b1v[:, e*32+hc*8+oc:e*32+hc*8+oc+1]), sig=ev)
                    pool.set_free(idx, ev, ev.n)
                w1_free[par] = (ev, ev.n)
            h1_done = ev.n
            for oc in range(8):
                par = w2i % 2
                w2i += 1
                s0, c0 = w2_free[par]
                if s0 is not None:
                    b.wait('gpsimd', s0, c0)
                b.dma('gpsimd', w2c[par][:], w2r[:, :, oc*128:(oc+1)*128], ldw)
                b.wait('tensor', ldw, ldw.n)
                b.wait('tensor', ev, h1_done)
                idx, bank = pool.acquire()
                for kt in range(32):
                    b.op('tensor', lambda te, kt=kt, bank=bank, par=par: te.matmul(
                        bank[:, 0:512], w2c[par][:, kt, :], h1[:, kt, :],
                        start=(kt == 0), stop=(kt == 31)), sig=(mm if kt == 31 else None))
                w2_free[par] = (mm, mm.n)
                b.wait('vector', mm, mm.n)
                b.wait('vector', ev, werep_done)
                b.op('vector', lambda v, bank=bank: v.tensor_tensor(
                    mtmp[:, :], bank[:, 0:512], werep[:, :], op=ALU.mult))
                b.op('vector', lambda v, oc=oc, tc=tc: v.tensor_tensor(
                    x[:, oc, tc*512:(tc+1)*512], x[:, oc, tc*512:(tc+1)*512],
                    mtmp[:, :], op=ALU.add), sig=ev)
                pool.set_free(idx, ev, ev.n)
    b.f0_dbg = dict(logits=logits, weT=weT, h1=h1, werep=werep, t=t)
    b.wait('tensor', ev, ev.n)
    for tc in range(4):
        for oc in range(8):
            idx, bank = pool.acquire()
            b.op('tensor', lambda te, bank=bank, oc=oc, tc=tc: te.matmul(
                bank[:, 0:512], b2bf[:, oc*128:(oc+1)*128], weTb[:, tc*512:(tc+1)*512],
                start=True, stop=True), sig=mm)
            b.wait('vector', mm, mm.n)
            b.op('vector', lambda v, bank=bank, oc=oc, tc=tc: v.tensor_tensor(
                x[:, oc, tc*512:(tc+1)*512], x[:, oc, tc*512:(tc+1)*512], bank[:, 0:512],
                op=ALU.add), sig=ev)
            pool.set_free(idx, ev, ev.n)
    barrier(b, ALL, [ev, mm])


def a0_dram(b, pfx):
    d = {}
    d['fv64'] = b.din(pfx+'fv64', [64, 32], F32)
    d['bvrow'] = b.din(pfx+'bvrow', [1, 1024], F32)
    d['fvout'] = b.din(pfx+'fvout', [128, 8], F32)
    d['wqkvh'] = b.din(pfx+'wqkvh', [128, 24576], F32)
    d['wouth'] = b.din(pfx+'wouth', [128, 8192], F32)
    return d


def a0_host_prep(I, pfx_in='L0_attn_'):
    g = lambda k: np.asarray(I[pfx_in + k], dtype=np.float32)
    lhsT = lambda w: np.ascontiguousarray(
        w.reshape(w.shape[0] // 128, 128, w.shape[1]).transpose(1, 0, 2).reshape(128, -1))
    out = {}
    b_qkv = g('b_qkv')
    fv64 = np.zeros((64, 32), np.float32)
    fv64[:, 0:16] = b_qkv[0:1024].reshape(16, 64).T
    fv64[:, 16:32] = b_qkv[1024:2048].reshape(16, 64).T
    out['fv64'] = fv64
    out['bvrow'] = np.ascontiguousarray(b_qkv[2048:3072].reshape(1, 1024))
    out['fvout'] = np.ascontiguousarray(g('b_out').reshape(8, 128).T)
    out['wqkvh'] = lhsT(np.ascontiguousarray(g('W_qkv').T))
    out['wouth'] = lhsT(np.ascontiguousarray(g('W_out').T))
    return out


def emit_a0(b, sh, pool, W, x, ar, mm, ev, ldw, T=2048, tag='a0', TOPK=614, TBIS=20):
    FT = 8
    NH = 16
    lam = 1.0 / 32.0
    K = lambda n: n * 1024
    ALL = ENGINES
    sh.make_ident_sel()
    barrier(b, ALL, [sh.init_sig])

    u = ar.tile([128, FT, T], BF16, K(0), f'{tag}_u')
    ctx = ar.tile([128, FT, T], BF16, K(32), f'{tag}_ctx')
    G = 4
    sc = [ar.tile([128, T], BF16, K(64 + 4 * i), f'{tag}_sc{i}') for i in range(G)]
    qh = ar.tile([64, T], BF16, K(80), f'{tag}_qh')
    kh = ar.tile([64, T], BF16, K(84), f'{tag}_kh')
    vtm = ar.tile([128, 16, 256], BF16, K(88), f'{tag}_vtm')
    eT = ar.tile([128, 8, 512], BF16, K(96), f'{tag}_eT')
    trep = ar.tile([128, T], BF16, K(104), f'{tag}_trep')
    zrow = ar.tile([1, 512], F32, K(108), f'{tag}_zrow')
    zinv = ar.tile([64, 512], F32, K(110), f'{tag}_zinv')
    wv = [ar.tile([128, 8, 256], BF16, K(112), f'{tag}_wv0')]
    wq = [ar.tile([128, 8, 64], BF16, K(116) + i * 1024, f'{tag}_wq{i}') for i in range(2)]
    wk = [ar.tile([128, 8, 64], BF16, K(118) + i * 1024, f'{tag}_wk{i}') for i in range(2)]
    wo = ar.tile([128, 8, 1024], BF16, K(64), f'{tag}_wo')
    msk = ar.tile([128, 512], BF16, K(120), f'{tag}_msk')
    otmp = ar.tile([128, 512], BF16, K(121), f'{tag}_otmp')
    trow = ar.tile([1, T], BF16, K(122), f'{tag}_trow')
    ttile = ar.tile([16, 128], BF16, K(124), f'{tag}_ttile')
    lo_s = b.sb(f'{tag}s_lo', [128, 16], F32)
    hi_s = b.sb(f'{tag}s_hi', [128, 16], F32)
    th_s = b.sb(f'{tag}s_th', [128, 16], F32)
    U32 = mybir.dt.uint32
    fnd_s = b.sb(f'{tag}s_fnd', [128, 16], U32)
    cnt_s = b.sb(f'{tag}s_cnt', [128, 16], F32)
    mid_s = b.sb(f'{tag}s_mid', [128, 16], F32)
    tau_s = b.sb(f'{tag}s_tau', [128, 16], F32)
    geM = b.sb(f'{tag}s_geM', [128, 16], U32)
    eqM = b.sb(f'{tag}s_eqM', [128, 16], U32)
    junk = b.sb(f'{tag}s_junk', [128, T], BF16)
    fv64 = b.sb(f'{tag}s_fv64', [64, 32], F32)
    bvrow = b.sb(f'{tag}s_bvrow', [1, 1024], BF16)
    fvout = b.sb(f'{tag}s_fvout', [128, 8], F32)
    psbf = b.psum(f'{tag}_psbf', [128, 512], BF16)
    zps = b.psum(f'{tag}_zps', [128, 512], F32)
    cps = b.psum(f'{tag}_cps', [128, 512], F32)
    zfree = (None, 0)
    cfree = (None, 0)

    b.dma('sync', fv64[:], W['fv64'][:, :], ldw)
    b.dma('gpsimd', bvrow[:], W['bvrow'][:, :], ldw)
    b.dma('sync', fvout[:], W['fvout'][:, :], ldw)
    barrier(b, ALL, [ldw])

    def lnscr(rows_off, cb_off):
        d = {}
        d['mean'] = ar.tile([1, 512], F32, rows_off)
        d['var'] = ar.tile([1, 512], F32, rows_off + K(2))
        d['t2'] = ar.tile([1, 512], F32, rows_off + K(4))
        d['tmp'] = ar.tile([128, 512], F32, rows_off + K(6))
        d['sq2'] = [ar.tile([128, FT, 512], BF16, cb_off + i * K(8)) for i in range(2)]
        d['xb2'] = [ar.tile([128, FT, 512], BF16, cb_off + K(16) + i * K(8)) for i in range(2)]
        return d

    emit_ln(b, sh, pool, lambda ft: x[:, ft, :], lambda ft, t0, t1: u[:, ft, t0:t1],
            FT, T, lnscr(K(32), K(64)), mm, ev)
    barrier(b, ALL, [ev, mm])

    qkvr = W['wqkvh'][:, :].rearrange('p (a c) -> p a c', c=3072)
    wq_free = [(None, 0)] * 2
    wk_free = [(None, 0)] * 2
    wv_free = [(None, 0)]

    for hd in range(NH):
        par = hd % 2
        g4 = hd // 4
        for (wfree, wbuf, c0_) in ((wq_free, wq[par], hd*64), (wk_free, wk[par], 1024+hd*64)):
            s0, c0 = wfree[par if wfree is not wv_free else 0]
            if s0 is not None:
                b.wait('gpsimd', s0, c0)
            b.dma('gpsimd', wbuf[:], qkvr[:, :, c0_:c0_+64], ldw)
        if hd % 4 == 0:
            s0, c0 = wv_free[0]
            if s0 is not None:
                b.wait('gpsimd', s0, c0)
            b.dma('gpsimd', wv[0][:], qkvr[:, :, 2048+g4*256:2048+(g4+1)*256], ldw)
        b.wait('tensor', ldw, ldw.n)

        for wbuf, dst, bcol in ((wq[par], qh, hd), (wk[par], kh, 16 + hd)):
            for tc in range(4):
                idx, bank = pool.acquire()
                for kt in range(FT):
                    b.op('tensor', lambda te, kt=kt, bank=bank, wbuf=wbuf, tc=tc: te.matmul(
                        bank[0:64, 0:512], wbuf[:, kt, :], u[:, kt, tc*512:(tc+1)*512],
                        start=(kt == 0), stop=(kt == FT - 1)),
                        sig=(mm if kt == FT - 1 else None))
                b.wait('scalar', mm, mm.n)
                b.op('scalar', lambda s, bank=bank, dst=dst, tc=tc, bcol=bcol: s.activation(
                    out=dst[:, tc*512:(tc+1)*512], in_=bank[0:64, 0:512], func=AF.Identity,
                    bias=fv64[:, bcol:bcol+1]), sig=ev)
                pool.set_free(idx, ev, ev.n)
        wq_free[par] = (ev, ev.n)
        wk_free[par] = (ev, ev.n)
        if hd % 4 == 0:
            for tc16 in range(16):
                idx, bank = pool.acquire()
                for kt in range(FT):
                    b.op('tensor', lambda te, kt=kt, bank=bank, tc16=tc16: te.matmul(
                        bank[:, 0:256], u[:, kt, tc16*128:(tc16+1)*128], wv[0][:, kt, :],
                        start=(kt == 0), stop=False))
                b.op('tensor', lambda te, bank=bank, g4=g4: te.matmul(
                    bank[:, 0:256], sh.ones_row_bf[:, :], bvrow[0:1, g4*256:(g4+1)*256],
                    start=False, stop=True), sig=mm)
                b.wait('vector', mm, mm.n)
                b.op('vector', lambda v, bank=bank, tc16=tc16: v.tensor_copy(
                    vtm[:, tc16, :], bank[:, 0:256]), sig=ev, dr=False)
                pool.set_free(idx, ev, ev.n)
            wv_free[0] = (ev, ev.n)
        vslice = (hd % 4) * 64
        b.wait('tensor', ev, ev.n)

        for grp in range(16 // G):
            b.wait('scalar', ev, ev.n)
            for gi in range(G):
                qt = grp * G + gi
                for kc4 in range(4):
                    idx, bank = pool.acquire()
                    b.op('tensor', lambda te, bank=bank, qt=qt, kc4=kc4: te.matmul(
                        bank[:, 0:512], qh[:, qt*128:(qt+1)*128], kh[:, kc4*512:(kc4+1)*512],
                        start=True, stop=True), sig=mm)
                    b.wait('scalar', mm, mm.n)
                    b.op('scalar', lambda s, bank=bank, gi=gi, kc4=kc4: s.activation(
                        out=sc[gi][:, kc4*512:(kc4+1)*512], in_=bank[:, 0:512],
                        func=AF.Identity), sig=ev)
                    pool.set_free(idx, ev, ev.n)
            sc_done = ev.n
            b.wait('vector', ev, sc_done)
            cb = grp * G
            b.op('vector', lambda v, cb=cb: v.memset(lo_s[:, cb:cb+G], -64.0), dr=False)
            b.op('vector', lambda v, cb=cb: v.memset(hi_s[:, cb:cb+G], 64.0), dr=False)
            b.op('vector', lambda v, cb=cb: v.memset(th_s[:, cb:cb+G], -64.0), dr=False)
            b.op('vector', lambda v, cb=cb: v.memset(fnd_s[:, cb:cb+G], 0))
            for it in range(TBIS):
                b.op('vector', lambda v, cb=cb: v.scalar_tensor_tensor(
                    mid_s[:, cb:cb+G], lo_s[:, cb:cb+G], 0.5, hi_s[:, cb:cb+G],
                    op0=ALU.mult, op1=ALU.bypass) if False else v.tensor_tensor(
                    mid_s[:, cb:cb+G], lo_s[:, cb:cb+G], hi_s[:, cb:cb+G], op=ALU.add), dr=False)
                b.op('vector', lambda v, cb=cb: v.tensor_scalar(
                    mid_s[:, cb:cb+G], mid_s[:, cb:cb+G], 0.5, None, op0=ALU.mult))
                for gi in range(G):
                    b.op('vector', lambda v, gi=gi, cb=cb: v.tensor_scalar(
                        junk[:, 0:T], sc[gi][:, :], mid_s[:, cb+gi:cb+gi+1], 0.0,
                        op0=ALU.is_ge, op1=ALU.add, accum_out=cnt_s[:, cb+gi:cb+gi+1]),
                        dr=(gi == G - 1))
                b.op('vector', lambda v, cb=cb: v.tensor_scalar(
                    geM[:, cb:cb+G], cnt_s[:, cb:cb+G], float(TOPK), None, op0=ALU.is_ge))
                b.op('vector', lambda v, cb=cb: v.tensor_scalar(
                    eqM[:, cb:cb+G], cnt_s[:, cb:cb+G], float(TOPK), None, op0=ALU.is_equal))
                b.op('vector', lambda v, cb=cb: v.select(
                    th_s[:, cb:cb+G], eqM[:, cb:cb+G], mid_s[:, cb:cb+G], th_s[:, cb:cb+G]))
                b.op('vector', lambda v, cb=cb: v.tensor_tensor(
                    fnd_s[:, cb:cb+G], fnd_s[:, cb:cb+G], eqM[:, cb:cb+G], op=ALU.max))
                b.op('vector', lambda v, cb=cb: v.select(
                    lo_s[:, cb:cb+G], geM[:, cb:cb+G], mid_s[:, cb:cb+G], lo_s[:, cb:cb+G]))
                b.op('vector', lambda v, cb=cb: v.select(
                    hi_s[:, cb:cb+G], geM[:, cb:cb+G], hi_s[:, cb:cb+G], mid_s[:, cb:cb+G]))
            b.op('vector', lambda v, cb=cb: v.select(
                th_s[:, cb:cb+G], fnd_s[:, cb:cb+G], th_s[:, cb:cb+G], lo_s[:, cb:cb+G]), sig=ev)
        b.wait('scalar', ev, ev.n)
        b.op('scalar', lambda s: s.activation(out=tau_s[:, :], in_=th_s[:, :],
                                              func=AF.Exp, scale=lam), sig=ev)
        b.wait('vector', ev, ev.n)
        b.op('vector', lambda v: v.tensor_copy(junk[:, 0:16], tau_s[:, :]), sig=ev)
        b.wait('tensor', ev, ev.n)
        b.op('tensor', lambda te: te.transpose(psbf[0:16, 0:128], junk[:, 0:16], sh.ident_bf[:, :]), sig=mm)
        b.wait('vector', mm, mm.n)
        b.op('vector', lambda v: v.tensor_copy(ttile[:, :], psbf[0:16, 0:128]), sig=ev)
        b.wait('tensor', ev, ev.n)
        for ti in range(16):
            idx, bank = pool.acquire()
            b.op('tensor', lambda te, bank=bank, ti=ti: te.matmul(
                bank[:, 0:128], sh.sel16_bf[:, ti, :], ttile[:, :],
                start=True, stop=True), sig=mm)
            b.wait('vector', mm, mm.n)
            b.op('vector', lambda v, bank=bank, ti=ti: v.tensor_copy(
                trep[:, ti*128:(ti+1)*128], bank[:, 0:128]), sig=ev, dr=False)
            pool.set_free(idx, ev, ev.n)
        b.wait('tensor', ev, ev.n)

        for qc in range(4):
            bz, bc = zps, cps
            if zfree[0] is not None:
                b.wait('tensor', zfree[0], zfree[1])
            if cfree[0] is not None:
                b.wait('tensor', cfree[0], cfree[1])
            for blk in range(2):
                b.wait('scalar', mm, mm.n)
                for j in range(8):
                    kc = blk * 8 + j
                    idx, bank = pool.acquire()
                    b.op('tensor', lambda te, bank=bank, kc=kc, qc=qc: te.matmul(
                        bank[:, 0:512], kh[:, kc*128:(kc+1)*128], qh[:, qc*512:(qc+1)*512],
                        start=True, stop=True), sig=mm)
                    b.wait('scalar', mm, mm.n)
                    b.op('scalar', lambda s, bank=bank, j=j: s.activation(
                        out=eT[:, j, :], in_=bank[:, 0:512], func=AF.Exp,
                        scale=lam), sig=ev)
                    pool.set_free(idx, ev, ev.n)
                    b.wait('vector', ev, ev.n)
                    b.op('vector', lambda v, j=j, qc=qc: v.tensor_tensor(
                        msk[:, :], eT[:, j, :], trep[:, qc*512:(qc+1)*512], op=ALU.is_ge), dr=False)
                    b.op('vector', lambda v, j=j: v.tensor_tensor(
                        eT[:, j, :], eT[:, j, :], msk[:, :], op=ALU.mult), sig=ev)
                b.wait('tensor', ev, ev.n)
                for j in range(8):
                    kc = blk * 8 + j
                    b.op('tensor', lambda te, j=j, kc=kc: te.matmul(
                        bz[0:1, 0:512], sh.ones_col_bf[:, :], eT[:, j, :],
                        start=(kc == 0), stop=(kc == 15)), sig=(mm if j == 7 else None))
                    b.op('tensor', lambda te, j=j, kc=kc, vslice=vslice: te.matmul(
                        bc[0:64, 0:512], vtm[:, kc, vslice:vslice+64], eT[:, j, :],
                        start=(kc == 0), stop=(kc == 15)), sig=(mm if j == 7 else None))
            b.wait('vector', mm, mm.n)
            b.op('vector', lambda v: v.reciprocal(zrow[0:1, :], bz[0:1, 0:512]), sig=ev)
            zfree = (ev, ev.n)
            b.wait('tensor', ev, ev.n)
            ib, bb = pool.acquire()
            b.op('tensor', lambda te, bb=bb: te.matmul(
                bb[0:64, 0:512], sh.ones_row_f32[0:1, 0:64], zrow[0:1, :],
                start=True, stop=True), sig=mm)
            b.wait('vector', mm, mm.n)
            b.op('vector', lambda v, bb=bb: v.tensor_copy(zinv[:, :], bb[0:64, 0:512]), sig=ev, dr=False)
            pool.set_free(ib, ev, ev.n)
            b.wait('vector', mm, mm.n)
            b.wait('vector', ev, ev.n)
            ftc, pbase = hd // 2, (hd % 2) * 64
            b.op('vector', lambda v, ftc=ftc, pbase=pbase, qc=qc: v.tensor_tensor(
                ctx[pbase:pbase+64, ftc, qc*512:(qc+1)*512], bc[0:64, 0:512], zinv[:, :],
                op=ALU.mult), sig=ev)
            cfree = (ev, ev.n)

    barrier(b, ALL, [ev, mm])
    b.dma('gpsimd', wo[:], W['wouth'][:, :].rearrange('p (a c) -> p a c', a=8), ldw)
    b.wait('tensor', ldw, ldw.n)
    for tc in range(4):
        for oc in range(FT):
            idx, bank = pool.acquire()
            for kt in range(FT):
                b.op('tensor', lambda te, kt=kt, bank=bank, oc=oc, tc=tc: te.matmul(
                    bank[:, 0:512], wo[:, kt, oc*128:(oc+1)*128], ctx[:, kt, tc*512:(tc+1)*512],
                    start=(kt == 0), stop=(kt == FT - 1)), sig=(mm if kt == FT - 1 else None))
            b.wait('scalar', mm, mm.n)
            b.wait('scalar', ev, ev.n)
            b.op('scalar', lambda s, bank=bank, oc=oc: s.activation(
                out=otmp[:, :], in_=bank[:, 0:512], func=AF.Identity,
                bias=fvout[:, oc:oc+1]), sig=ev)
            b.wait('vector', ev, ev.n)
            b.op('vector', lambda v, oc=oc, tc=tc: v.tensor_tensor(
                x[:, oc, tc*512:(tc+1)*512], x[:, oc, tc*512:(tc+1)*512], otmp[:, :],
                op=ALU.add), sig=ev)
            pool.set_free(idx, ev, ev.n)
    barrier(b, ALL, [ev, mm])


def f1_dram(b, pfx):
    d = {}
    d['wqkvh'] = b.din(pfx+'wqkvh', [2, 128, 8*1536], F32)
    d['woh'] = b.din(pfx+'woh', [2, 128, 4*1024], F32)
    d['w1h'] = b.din(pfx+'w1h', [2, 128, 8*5632], F32)
    d['w2h'] = b.din(pfx+'w2h', [2, 128, 22*1024], F32)
    d['fv'] = b.din(pfx+'fv', [128, 2*18], F32)
    d['fv2'] = b.din(pfx+'fv2', [128, 2*52], F32)
    d['qs'] = b.din(pfx+'qs', [64, 4], F32)
    d['rot'] = b.din(pfx+'rot', [64, 64], F32)
    d['ab'] = b.din(pfx+'ab', [64, 4*2048], F32)
    d['maskq'] = b.din(pfx+'maskq', [128, 4*1024], F32)
    d['maskt'] = b.din(pfx+'maskt', [128, 8*512], F32)
    return d


def f1_host_prep(I, pfx_in='L1_ffn_'):
    g = lambda k: np.asarray(I[pfx_in + k], dtype=np.float32)
    lhsT = lambda w: np.ascontiguousarray(
        w.reshape(w.shape[0] // 128, 128, w.shape[1]).transpose(1, 0, 2).reshape(128, -1))
    fvec = lambda v: np.ascontiguousarray(v.reshape(-1, 128).T)
    out = {}
    out['wqkvh'] = np.stack([lhsT(g('wqkv')[l]) for l in range(2)])
    out['woh'] = np.stack([lhsT(g('wo')[l]) for l in range(2)])
    w1p = np.zeros((2, 1024, 5632), np.float32)
    w1p[:, :, 0:2730] = g('w1')[:, :, 0:2730]
    w1p[:, :, 2816:5546] = g('w1')[:, :, 2730:5460]
    out['w1h'] = np.stack([lhsT(w1p[l]) for l in range(2)])
    w2p = np.zeros((2, 2816, 1024), np.float32)
    w2p[:, 0:2730, :] = g('w2')
    out['w2h'] = np.stack([lhsT(w2p[l]) for l in range(2)])
    fv = np.zeros((128, 36), np.float32)
    for l in range(2):
        fv[:, l*18:l*18+8] = fvec(g('ln_g')[l])
        fv[:, l*18+8:l*18+16] = fvec(g('ln_b')[l])
    out['fv'] = fv
    fv2 = np.zeros((128, 104), np.float32)
    for l in range(2):
        b1p = np.zeros(5632, np.float32)
        b1p[0:2730] = g('b1')[l][0:2730]
        b1p[2816:5546] = g('b1')[l][2730:5460]
        fv2[:, l*52:l*52+44] = fvec(b1p)
        fv2[:, l*52+44:l*52+52] = fvec(g('b2')[l])
    out['fv2'] = fv2
    qs = np.zeros((64, 4), np.float32)
    for l in range(2):
        qs[:, l*2+0] = g('q_scale')[l]
        qs[:, l*2+1] = g('k_scale')[l]
    out['qs'] = qs
    R = np.zeros((64, 64), np.float32)
    for dd in range(32):
        R[dd+32, dd] = -1.0
        R[dd, dd+32] = 1.0
    out['rot'] = R
    DIM_HEAD, WINDOW, n = 64, 512, 2048
    half_idx = np.arange(0, DIM_HEAD, 2, dtype=np.float32)
    inv_freq = 1.0 / 10000.0 ** (half_idx / DIM_HEAD)
    t = np.arange(n, dtype=np.float32)
    freqs = t[:, None] * inv_freq[None, :]
    freqs = np.concatenate([freqs, freqs], -1)
    base = (half_idx + 0.4 * DIM_HEAD) / (1.4 * DIM_HEAD)
    power = (t - n // 2) / float(WINDOW // 2)
    scale = base[None, :] ** power[:, None]
    scale = np.concatenate([scale, scale], -1)
    cos, sin = np.cos(freqs), np.sin(freqs)
    ab = np.zeros((64, 4*2048), np.float32)
    ab[:, 0*2048:1*2048] = (cos * scale).T
    ab[:, 1*2048:2*2048] = (sin * scale).T
    ab[:, 2*2048:3*2048] = (cos / scale).T
    ab[:, 3*2048:4*2048] = (sin / scale).T
    out['ab'] = ab
    i_loc = np.arange(512)
    j_loc = np.arange(1024)
    allowed = (j_loc[None, :] >= i_loc[:, None]) & (j_loc[None, :] <= i_loc[:, None] + 512)
    bias = np.where(allowed, 0.0, -30000.0).astype(np.float32)
    mq = np.zeros((128, 4*1024), np.float32)
    for qc in range(4):
        mq[:, qc*1024:(qc+1)*1024] = bias[qc*128:(qc+1)*128, :]
    out['maskq'] = mq
    mt = np.zeros((128, 8*512), np.float32)
    for kc in range(8):
        mt[:, kc*512:(kc+1)*512] = bias[:, kc*128:(kc+1)*128].T
    out['maskt'] = mt
    return out


def emit_f1(b, sh, pool, W, x, ar, mm, ev, ldw, T=2048, coef=0.5, tag='f1', layers=2, ffn_on=True):
    FT = 8
    NH = 8
    K = lambda n: n * 1024
    ALL = ENGINES
    sh.make_ident_sel()
    barrier(b, ALL, [sh.init_sig])

    xi = ar.tile([128, FT, T], BF16, K(0), f'{tag}_xi')
    hs = ar.tile([128, FT, T], BF16, K(32), f'{tag}_hs')
    qkfm = ar.tile([128, 8, T], BF16, K(64), f'{tag}_qkfm')
    vtm = ar.tile([128, 16, 512], BF16, K(80), f'{tag}_vtm')
    ctx = ar.tile([128, 4, T], BF16, K(96), f'{tag}_ctx')
    abt = ar.tile([64, 4, T], BF16, K(112), f'{tag}_abt')
    qf = ar.tile([64, T], BF16, K(32), f'{tag}_qf')
    kf = ar.tile([64, T], BF16, K(36), f'{tag}_kf')
    q1 = ar.tile([64, T], BF16, K(40), f'{tag}_q1')
    rots = ar.tile([64, T], BF16, K(44), f'{tag}_rots')
    rrow = ar.tile([1, T], F32, K(44), f'{tag}_rrow')
    maskq = ar.tile([128, 4, 1024], BF16, K(44), f'{tag}_maskq')
    q0 = ar.tile([64, T], BF16, K(48), f'{tag}_q0')
    maskt = ar.tile([128, 8, 512], BF16, K(52), f'{tag}_maskt')
    srow = ar.tile([128, 1024], BF16, K(60), f'{tag}_srow')
    mrep = ar.tile([128, 512], F32, K(62), f'{tag}_mrep')
    eT = ar.tile([128, 4, 512], BF16, K(128), f'{tag}_eT')
    wo_t = ar.tile([128, 4, 1024], BF16, K(32), f'{tag}_wo')
    tff = ar.tile([128, FT, T], BF16, K(96), f'{tag}_tff')
    h1q = ar.tile([128, 44, 512], BF16, K(32), f'{tag}_h1q')
    w1c = [ar.tile([128, 8, 128], BF16, K(76) + i * 2048, f'{tag}_w1c{i}') for i in range(2)]
    w2c = [ar.tile([128, 22, 128], BF16, K(80) + i * 6144, f'{tag}_w2c{i}') for i in range(2)]
    otmp = b.sb(f'{tag}s_otmp', [128, 512], BF16)
    m_all = b.sb(f'{tag}s_mall', [128, 4], F32)
    mbf = b.sb(f'{tag}s_mbf', [128, 16], BF16)
    mt4 = b.sb(f'{tag}s_mt4', [16, 128], BF16)
    ssq = ar.tile([1, T], F32, K(52), f'{tag}_ssqp')
    zrowf = ssq
    fv = b.sb(f'{tag}s_fv', [128, 36], F32)
    fv2 = b.sb(f'{tag}s_fv2', [128, 104], F32)
    qs_t = b.sb(f'{tag}s_qs', [64, 4], F32)
    rot_t = b.sb(f'{tag}s_rot', [64, 64], BF16)
    psbf = b.psum(f'{tag}_psbf', [128, 512], BF16)
    zps = b.psum(f'{tag}_zps', [128, 512], F32)
    cps = b.psum(f'{tag}_cps', [128, 512], F32)
    zfree = (None, 0)
    cfree = (None, 0)

    b.dma('sync', fv[:], W['fv'][:, :], ldw)
    b.dma('sync', fv2[:], W['fv2'][:, :], ldw)
    b.dma('sync', qs_t[:], W['qs'][:, :], ldw)
    b.dma('gpsimd', rot_t[:], W['rot'][:, :], ldw)
    barrier(b, ALL, [ldw])

    def lnscr(rows_off, cb_off, with_xb=False):
        d = {}
        d['mean'] = ar.tile([1, 512], F32, rows_off)
        d['var'] = ar.tile([1, 512], F32, rows_off + K(2))
        d['t2'] = ar.tile([1, 512], F32, rows_off + K(4))
        d['tmp'] = ar.tile([128, 512], F32, rows_off + K(6))
        d['sq2'] = [ar.tile([128, FT, 512], BF16, cb_off + i * K(8)) for i in range(2)]
        d['xb2'] = ([ar.tile([128, FT, 512], BF16, cb_off + K(16) + i * K(8)) for i in range(2)]
                    if with_xb else [None, None])
        return d

    emit_ln(b, sh, pool, lambda ft: x[:, ft, :], lambda ft, t0, t1: xi[:, ft, t0:t1],
            FT, T, lnscr(K(32), K(64), True), mm, ev)
    barrier(b, ALL, [ev, mm])

    def shift(src_fn, dst):
        for ft in range(4):
            b.op('vector', lambda v, ft=ft: v.tensor_copy(dst[:, ft, :], src_fn(ft)), dr=False)
        for ft in range(4, 8):
            b.op('vector', lambda v, ft=ft: v.tensor_copy(dst[:, ft, 1:T], src_fn(ft)[:, 0:T-1]), dr=False)
            b.op('vector', lambda v, ft=ft: v.memset(dst[:, ft, 0:1], 0.0), dr=False)
        b.op('vector', lambda v: v.engine_nop(), sig=ev)

    for L in range(layers):
        qkvr = W['wqkvh'][L].rearrange('p (a c) -> p a c', a=8)
        wor = W['woh'][L].rearrange('p (a c) -> p a c', a=4)
        w1r = W['w1h'][L].rearrange('p (a c) -> p a c', a=8)
        w2r = W['w2h'][L].rearrange('p (a c) -> p a c', a=22)
        shift(lambda ft: xi[:, ft, :], hs)
        barrier(b, ALL, [ev, mm])
        b.dma('gpsimd', abt[:], W['ab'][:, :].rearrange('p (a c) -> p a c', a=4), ldw)
        wqk = [ar.tile([128, 8, 128], BF16, K(80) + i * 2048, f'{tag}_wqk{L}_{i}') for i in range(2)]
        wfree = [(None, 0)] * 2
        import os as _os3
        _ocs = list(range(8)) if not _os3.environ.get('F1SWAP') else list(range(4, 8)) + list(range(4))
        for oc in _ocs:
            par = oc % 2
            s0, c0 = wfree[par]
            if s0 is not None:
                b.wait('gpsimd', s0, c0)
            b.dma('gpsimd', wqk[par][:], qkvr[:, :, oc*128:(oc+1)*128], ldw)
            b.wait('tensor', ldw, ldw.n)
            for tc in range(4):
                idx, bank = pool.acquire()
                for kt in range(FT):
                    b.op('tensor', lambda te, kt=kt, bank=bank, par=par, tc=tc: te.matmul(
                        bank[:, 0:512], wqk[par][:, kt, :], hs[:, kt, tc*512:(tc+1)*512],
                        start=(kt == 0), stop=(kt == FT - 1)), sig=(mm if kt == FT - 1 else None))
                b.wait('scalar', mm, mm.n)
                b.op('scalar', lambda s, bank=bank, oc=oc, tc=tc: s.activation(
                    out=qkfm[:, oc, tc*512:(tc+1)*512], in_=bank[:, 0:512], func=AF.Identity), sig=ev)
                pool.set_free(idx, ev, ev.n)
            wfree[par] = (ev, ev.n)
        import os as _os4
        if _os4.environ.get('F1DBGQK') and L == 0:
            dqk0 = b.dout(f'{tag}_dqk0', [1024, T], F32)
            _st2 = b.sem(f'{tag}dbgst2')
            barrier(b, ['gpsimd'], [ev, mm])
            for ft in range(8):
                b.dma('gpsimd', dqk0[ft*128:(ft+1)*128, :], qkfm[:, ft, :], _st2)
            b.wait('gpsimd', _st2, _st2.n)
            barrier(b, ALL, [_st2])
        wv_t = ar.tile([128, 8, 512], BF16, K(88), f'{tag}_wv{L}')
        wv_t = ar.tile([128, 8, 512], BF16, K(96), f'{tag}_wvb{L}')
        b.dma('gpsimd', wv_t[:], qkvr[:, :, 1024:1536], ldw)
        b.wait('tensor', ldw, ldw.n)
        for tc16 in range(16):
            idx, bank = pool.acquire()
            for kt in range(FT):
                b.op('tensor', lambda te, kt=kt, bank=bank, tc16=tc16: te.matmul(
                    bank[:, 0:512], hs[:, kt, tc16*128:(tc16+1)*128], wv_t[:, kt, :],
                    start=(kt == 0), stop=(kt == FT - 1)), sig=(mm if kt == FT - 1 else None))
            b.wait('vector', mm, mm.n)
            b.op('vector', lambda v, bank=bank, tc16=tc16: v.tensor_copy(
                vtm[:, tc16, :], bank[:, 0:512]), sig=ev, dr=False)
            pool.set_free(idx, ev, ev.n)
        barrier(b, ALL, [ev, mm, ldw])
        import os as _os
        if _os.environ.get('F1DBGAB') and L == 0:
            dab0 = b.dout(f'{tag}_dab0', [64, 4, T], F32)
            _st = b.sem(f'{tag}dbgst')
            b.dma('gpsimd', dab0[:, :, :], abt[:], _st)
            b.wait('gpsimd', _st, _st.n)
            barrier(b, ALL, [_st])

        for hd in range(NH):
            b.wait('scalar', mm, mm.n)
            b.wait('scalar', ev, ev.n)
            b.wait('vector', mm, mm.n)
            b.wait('gpsimd', ev, ev.n)
            b.wait('gpsimd', mm, mm.n)
            fq, pq = (hd * 64) // 128, (hd * 64) % 128
            fk, pk = 4 + fq, pq
            qsl = qkfm[pq:pq+64, fq, :]
            ksl = qkfm[pk:pk+64, fk, :]
            for (src, dst, qcol, acol) in ((qsl, qf, 2*L+0, 0), (ksl, kf, 2*L+1, 2)):
                b.op('vector', lambda v, src=src: v.tensor_copy(q0[:, :], src))
                b.op('vector', lambda v, qcol=qcol: v.tensor_scalar(
                    q1[:, :], q0[:, :], qs_t[:, qcol:qcol+1], None, op0=ALU.mult), sig=ev)
                b.wait('tensor', ev, ev.n)
                for tc in range(4):
                    idx, bank = pool.acquire()
                    b.op('tensor', lambda te, bank=bank, tc=tc: te.matmul(
                        bank[0:64, 0:512], rot_t[:, :], q1[:, tc*512:(tc+1)*512],
                        start=True, stop=True), sig=mm)
                    b.wait('scalar', mm, mm.n)
                    b.op('scalar', lambda s, bank=bank, tc=tc: s.activation(
                        out=rots[:, tc*512:(tc+1)*512], in_=bank[0:64, 0:512], func=AF.Identity), sig=ev)
                    pool.set_free(idx, ev, ev.n)
                b.wait('vector', ev, ev.n)
                b.op('vector', lambda v, acol=acol: v.tensor_tensor(
                    dst[:, :] if False else q1[:, :], q1[:, :], abt[:, acol, :], op=ALU.mult), dr=False)
                b.op('vector', lambda v, acol=acol, dst=dst: v.tensor_tensor(
                    dst[:, :], rots[:, :], abt[:, acol+1, :], op=ALU.mult))
                b.op('vector', lambda v, dst=dst: v.tensor_tensor(
                    dst[:, :], dst[:, :], q1[:, :], op=ALU.add), sig=ev)
                b.wait('scalar', ev, ev.n)
                b.op('scalar', lambda s: s.activation(
                    out=rots[:, :], in_=q0[:, :], func=AF.Square), sig=ev)
                b.wait('tensor', ev, ev.n)
                for tc in range(4):
                    idx, bank = pool.acquire()
                    b.op('tensor', lambda te, bank=bank, tc=tc: te.matmul(
                        bank[0:1, 0:512], sh.ones_col_bf[0:64, :], rots[:, tc*512:(tc+1)*512],
                        start=True, stop=True), sig=mm)
                    b.wait('scalar', mm, mm.n)
                    b.op('scalar', lambda s, bank=bank, tc=tc: s.activation(
                        out=ssq[0:1, tc*512:(tc+1)*512], in_=bank[0:1, 0:512], func=AF.Sqrt), sig=ev)
                    pool.set_free(idx, ev, ev.n)
                b.wait('vector', ev, ev.n)
                b.op('vector', lambda v: v.tensor_scalar(
                    rrow[0:1, :], ssq[0:1, :], 1e-12, None, op0=ALU.max), dr=False)
                b.op('vector', lambda v: v.reciprocal(rrow[0:1, :], rrow[0:1, :]), sig=ev)
                b.wait('tensor', ev, ev.n)
                for tc in range(4):
                    idx, bank = pool.acquire()
                    b.op('tensor', lambda te, bank=bank, tc=tc: te.matmul(
                        bank[0:64, 0:512], sh.ones_row_f32[0:1, 0:64], rrow[0:1, tc*512:(tc+1)*512],
                        start=True, stop=True), sig=mm)
                    b.wait('vector', mm, mm.n)
                    b.op('vector', lambda v, bank=bank, tc=tc, dst=dst: v.tensor_tensor(
                        dst[:, tc*512:(tc+1)*512], dst[:, tc*512:(tc+1)*512], bank[0:64, 0:512],
                        op=ALU.mult), sig=ev)
                    pool.set_free(idx, ev, ev.n)
            b.wait('gpsimd', ev, ev.n)
            b.dma('gpsimd', maskq[:], W['maskq'][:, :].rearrange('p (a c) -> p a c', a=4), ldw)
            b.dma('gpsimd', maskt[:], W['maskt'][:, :].rearrange('p (a c) -> p a c', a=8), ldw)
            barrier(b, ['vector', 'scalar', 'tensor'], [ldw])
            ftc, pbase = hd // 2, (hd % 2) * 64
            for w in range(4):
                kcs = list(range(8)) if w > 0 else list(range(4, 8))
                key0 = (w - 1) * 512
                for qc in range(4):
                    qoff = w * 512 + qc * 128
                    nk = len(kcs) * 128
                    banks = []
                    for half in range(len(kcs) // 4):
                        idx, bank = pool.acquire()
                        koff = key0 + kcs[half*4] * 128
                        b.op('tensor', lambda te, bank=bank, qoff=qoff, koff=koff: te.matmul(
                            bank[:, 0:512], qf[:, qoff:qoff+128], kf[:, koff:koff+512],
                            start=True, stop=True), sig=mm)
                        banks.append((idx, bank, half))
                    b.wait('vector', mm, mm.n)
                    for (idx, bank, half) in banks:
                        mq0 = (1024 - nk) + half * 512
                        b.op('vector', lambda v, bank=bank, qc=qc, half=half, mq0=mq0: v.scalar_tensor_tensor(
                            srow[:, half*512:(half+1)*512], bank[:, 0:512], 8.0,
                            maskq[:, qc, mq0:mq0+512], op0=ALU.mult, op1=ALU.add), dr=False)
                    b.op('vector', lambda v: v.engine_nop(), sig=ev)
                    for (idx, bank, half) in banks:
                        pool.set_free(idx, ev, ev.n)
                    b.op('vector', lambda v, qc=qc, nk=nk: v.reduce_max(
                        m_all[:, qc:qc+1], srow[:, 0:nk], axis=AX.X), sig=ev)
                b.op('vector', lambda v: v.tensor_copy(mbf[:, 0:4], m_all[:, :]), sig=ev)
                b.wait('tensor', ev, ev.n)
                b.op('tensor', lambda te: te.transpose(psbf[0:4, 0:128], mbf[:, 0:4], sh.ident_bf[:, :]), sig=mm)
                b.wait('vector', mm, mm.n)
                b.op('vector', lambda v: v.tensor_copy(mt4[0:4, :], psbf[0:4, 0:128]), sig=ev)
                b.wait('tensor', ev, ev.n)
                idxm, bankm = pool.acquire()
                for qc in range(4):
                    b.op('tensor', lambda te, bankm=bankm, qc=qc: te.matmul(
                        bankm[:, qc*128:(qc+1)*128], sh.sel16_bf[0:4, qc, :], mt4[0:4, :],
                        start=True, stop=True), sig=(mm if qc == 3 else None))
                b.wait('vector', mm, mm.n)
                b.op('vector', lambda v, bankm=bankm: v.tensor_copy(mrep[:, :], bankm[:, 0:512]), sig=ev)
                pool.set_free(idxm, ev, ev.n)
                if zfree[0] is not None:
                    b.wait('tensor', zfree[0], zfree[1])
                if cfree[0] is not None:
                    b.wait('tensor', cfree[0], cfree[1])
                nj = len(kcs)
                for blk in range(nj // 4):
                    b.wait('scalar', mm, mm.n)
                    b.wait('scalar', ev, ev.n)
                    for jj in range(4):
                        j = blk * 4 + jj
                        kc = kcs[j]
                        koff = key0 + kc * 128
                        qoff = w * 512
                        idx, bank = pool.acquire()
                        b.op('tensor', lambda te, bank=bank, koff=koff, qoff=qoff: te.matmul(
                            bank[:, 0:512], kf[:, koff:koff+128], qf[:, qoff:qoff+512],
                            start=True, stop=True), sig=mm)
                        b.wait('vector', mm, mm.n)
                        b.op('vector', lambda v, bank=bank, kc=kc, jj=jj: v.scalar_tensor_tensor(
                            eT[:, jj, :], bank[:, 0:512], 8.0, maskt[:, kc, :],
                            op0=ALU.mult, op1=ALU.add), dr=False)
                        b.op('vector', lambda v, jj=jj: v.tensor_tensor(
                            eT[:, jj, :], eT[:, jj, :], mrep[:, :], op=ALU.subtract), sig=ev)
                        pool.set_free(idx, ev, ev.n)
                        b.wait('scalar', ev, ev.n)
                        b.op('scalar', lambda s, jj=jj: s.activation(
                            out=eT[:, jj, :], in_=eT[:, jj, :], func=AF.Exp), sig=ev)
                    b.wait('tensor', ev, ev.n)
                    for jj in range(4):
                        j = blk * 4 + jj
                        kc = kcs[j]
                        vt = (w - 1) * 4 + kc
                        b.op('tensor', lambda te, jj=jj, j=j, nj=nj: te.matmul(
                            zps[0:1, 0:512], sh.ones_col_bf[:, :], eT[:, jj, :],
                            start=(j == 0), stop=(j == nj - 1)), sig=(mm if jj == 3 else None))
                        b.op('tensor', lambda te, jj=jj, j=j, nj=nj, vt=vt, hd=hd: te.matmul(
                            cps[0:64, 0:512], vtm[:, vt, hd*64:hd*64+64], eT[:, jj, :],
                            start=(j == 0), stop=(j == nj - 1)), sig=(mm if jj == 3 else None))
                b.wait('vector', mm, mm.n)
                b.op('vector', lambda v: v.reciprocal(zrowf[0:1, :], zps[0:1, 0:512]), sig=ev)
                zfree = (ev, ev.n)
                b.wait('tensor', ev, ev.n)
                ib, bb = pool.acquire()
                b.op('tensor', lambda te, bb=bb: te.matmul(
                    bb[0:64, 0:512], sh.ones_row_f32[0:1, 0:64], zrowf[0:1, :],
                    start=True, stop=True), sig=mm)
                b.wait('vector', mm, mm.n)
                b.op('vector', lambda v, bb=bb: v.tensor_copy(mrep[:, :], bb[:, 0:512]), sig=ev)
                pool.set_free(ib, ev, ev.n)
                b.op('vector', lambda v, ftc=ftc, pbase=pbase, w=w: v.tensor_tensor(
                    ctx[pbase:pbase+64, ftc, w*512:(w+1)*512], cps[0:64, 0:512], mrep[0:64, 0:512],
                    op=ALU.mult), sig=ev)
                cfree = (ev, ev.n)
        barrier(b, ALL, [ev, mm])
        import os as _os2
        if _os2.environ.get('NOPROJ'):
            continue
        b.dma('gpsimd', wo_t[:], wor, ldw)
        b.wait('tensor', ldw, ldw.n)
        for tc in range(4):
            for oc in range(FT):
                idx, bank = pool.acquire()
                for kt in range(4):
                    b.op('tensor', lambda te, kt=kt, bank=bank, oc=oc, tc=tc: te.matmul(
                        bank[:, 0:512], wo_t[:, kt, oc*128:(oc+1)*128], ctx[:, kt, tc*512:(tc+1)*512],
                        start=(kt == 0), stop=(kt == 3)), sig=(mm if kt == 3 else None))
                b.wait('vector', mm, mm.n)
                b.op('vector', lambda v, bank=bank, oc=oc, tc=tc: v.tensor_tensor(
                    xi[:, oc, tc*512:(tc+1)*512], xi[:, oc, tc*512:(tc+1)*512], bank[:, 0:512],
                    op=ALU.add), sig=ev)
                pool.set_free(idx, ev, ev.n)
        barrier(b, ALL, [ev, mm])
        if not ffn_on:
            continue
        shift(lambda ft: xi[:, ft, :], hs)
        barrier(b, ALL, [ev, mm])
        emit_ln(b, sh, pool, lambda ft: hs[:, ft, :], lambda ft, t0, t1: tff[:, ft, t0:t1],
                FT, T, lnscr(K(76), K(32), False), mm, ev,
                gamma=fv[:, L*18:L*18+8], beta=fv[:, L*18+8:L*18+16], in_bf16=True)
        barrier(b, ALL, [ev, mm])
        w1free = [(None, 0)] * 2
        w2free = [(None, 0)] * 2
        wi = w2i = 0
        for tq in range(4):
            for oc in range(44):
                par = wi % 2
                wi += 1
                s0, c0 = w1free[par]
                if s0 is not None:
                    b.wait('gpsimd', s0, c0)
                b.dma('gpsimd', w1c[par][:], w1r[:, :, oc*128:(oc+1)*128], ldw)
                b.wait('tensor', ldw, ldw.n)
                idx, bank = pool.acquire()
                for kt in range(FT):
                    b.op('tensor', lambda te, kt=kt, bank=bank, par=par, tq=tq: te.matmul(
                        bank[:, 0:512], w1c[par][:, kt, :], tff[:, kt, tq*512:(tq+1)*512],
                        start=(kt == 0), stop=(kt == FT - 1)), sig=(mm if kt == FT - 1 else None))
                b.wait('scalar', mm, mm.n)
                fn = AF.Identity if oc < 22 else AF.Gelu
                b.op('scalar', lambda s, bank=bank, oc=oc, fn=fn, L=L: s.activation(
                    out=h1q[:, oc, :], in_=bank[:, 0:512], func=fn,
                    bias=fv2[:, L*52+oc:L*52+oc+1]), sig=ev)
                pool.set_free(idx, ev, ev.n)
                w1free[par] = (ev, ev.n)
            b.wait('vector', ev, ev.n)
            for j in range(22):
                b.op('vector', lambda v, j=j: v.tensor_tensor(
                    h1q[:, j, :], h1q[:, j, :], h1q[:, 22+j, :], op=ALU.mult),
                    dr=(j == 21), sig=(ev if j == 21 else None))
            for oc2 in range(8):
                par = w2i % 2
                w2i += 1
                s0, c0 = w2free[par]
                if s0 is not None:
                    b.wait('gpsimd', s0, c0)
                b.dma('gpsimd', w2c[par][:], w2r[:, :, oc2*128:(oc2+1)*128], ldw)
                b.wait('tensor', ldw, ldw.n)
                b.wait('tensor', ev, ev.n)
                idx, bank = pool.acquire()
                for kt in range(22):
                    b.op('tensor', lambda te, kt=kt, bank=bank, par=par: te.matmul(
                        bank[:, 0:512], w2c[par][:, kt, :], h1q[:, kt, :],
                        start=(kt == 0), stop=(kt == 21)), sig=(mm if kt == 21 else None))
                w2free[par] = (mm, mm.n)
                b.wait('scalar', mm, mm.n)
                b.wait('scalar', ev, ev.n)
                b.op('scalar', lambda s, bank=bank, oc2=oc2, L=L: s.activation(
                    out=otmp[:, :], in_=bank[:, 0:512], func=AF.Identity,
                    bias=fv2[:, L*52+44+oc2:L*52+44+oc2+1]), sig=ev)
                b.wait('vector', ev, ev.n)
                b.op('vector', lambda v, oc2=oc2, tq=tq: v.tensor_tensor(
                    xi[:, oc2, tq*512:(tq+1)*512], xi[:, oc2, tq*512:(tq+1)*512], otmp[:, :],
                    op=ALU.add), sig=ev)
                pool.set_free(idx, ev, ev.n)
        barrier(b, ALL, [ev, mm])
    for ft in range(FT):
        b.op('vector', lambda v, ft=ft: v.scalar_tensor_tensor(
            x[:, ft, :], xi[:, ft, :], coef, x[:, ft, :], op0=ALU.mult, op1=ALU.add), sig=ev)
    barrier(b, ALL, [ev, mm])
    b.f1_dbg = dict(xi=xi, ctx=ctx, qf=qf, kf=kf, qkfm=qkfm, vtm=vtm, hs=hs, m_all=m_all, mrep=mrep, eT=eT, mt4=mt4, srow=srow, q1=q1, rrow=rrow, abt=abt, ssq=ssq)


def emit_f1(b, sh, pool, W, x, ar, mm, ev, ldw, T=2048, coef=0.5, tag='f1'):
    FT = 8
    NH = 8
    K = lambda n: n * 1024
    ALL = ENGINES
    sh.make_ident_sel()
    barrier(b, ALL, [sh.init_sig])

    xi = ar.tile([128, FT, T], BF16, K(0), f'{tag}_xi')
    hs = ar.tile([128, FT, T], BF16, K(32), f'{tag}_hs')
    qkfm = ar.tile([128, 8, T], BF16, K(64), f'{tag}_qkfm')
    vtm = ar.tile([128, 16, 512], BF16, K(96), f'{tag}_vtm')
    abt = ar.tile([64, 4, T], BF16, K(32), f'{tag}_abt')
    q1 = ar.tile([64, T], BF16, K(48), f'{tag}_q1')
    rots = ar.tile([64, T], BF16, K(52), f'{tag}_rots')
    q2 = ar.tile([64, T], BF16, K(56), f'{tag}_q2')
    maskq = ar.tile([128, 4, 1024], BF16, K(48), f'{tag}_maskq')
    maskt = ar.tile([128, 8, 512], BF16, K(56), f'{tag}_maskt')
    wo_t = ar.tile([128, 4, 1024], BF16, K(32), f'{tag}_wo')
    ctx4 = ar.tile([128, 4, 512], BF16, K(112), f'{tag}_ctx4')
    srow = ar.tile([128, 1024], BF16, K(116), f'{tag}_srow')
    mrep = ar.tile([128, 512], F32, K(118), f'{tag}_mrep')
    eT = ar.tile([128, 8, 512], BF16, K(120), f'{tag}_eT')
    h1q = ar.tile([128, 44, 512], BF16, K(32), f'{tag}_h1q')
    w1c = [ar.tile([128, 8, 128], BF16, K(76) + i * 2048, f'{tag}_w1c{i}') for i in range(2)]
    w2c = [ar.tile([128, 22, 128], BF16, K(80) + i * 6144, f'{tag}_w2c{i}') for i in range(2)]
    tff = ar.tile([128, FT, T], BF16, K(96), f'{tag}_tff')
    otmp = b.sb(f'{tag}s_otmp', [128, 512], BF16)
    m_all = b.sb(f'{tag}s_mall', [128, 4], F32)
    mbf = b.sb(f'{tag}s_mbf', [128, 4], BF16)
    mt4 = b.sb(f'{tag}s_mt4', [16, 128], BF16)
    ssq = b.sb(f'{tag}s_ssq', [1, 512], F32)
    rrow = b.sb(f'{tag}s_rrow', [1, 512], F32)
    zrowf = ssq
    fv = b.sb(f'{tag}s_fv', [128, 36], F32)
    fv2 = b.sb(f'{tag}s_fv2', [128, 104], F32)
    qs_t = b.sb(f'{tag}s_qs', [64, 4], F32)
    rot_t = b.sb(f'{tag}s_rot', [64, 64], BF16)
    psbf = b.psum(f'{tag}_psbf', [128, 512], BF16)
    zps = b.psum(f'{tag}_zps', [128, 512], F32)
    cps = b.psum(f'{tag}_cps', [128, 512], F32)
    zfree = (None, 0)
    cfree = (None, 0)

    b.dma('sync', fv[:], W['fv'][:, :], ldw)
    b.dma('sync', fv2[:], W['fv2'][:, :], ldw)
    b.dma('sync', qs_t[:], W['qs'][:, :], ldw)
    b.dma('gpsimd', rot_t[:], W['rot'][:, :], ldw)
    barrier(b, ALL, [ldw])

    def lnscr(rows_off, cb_off, with_xb=False):
        d = {}
        d['mean'] = ar.tile([1, 512], F32, rows_off)
        d['var'] = ar.tile([1, 512], F32, rows_off + K(2))
        d['t2'] = ar.tile([1, 512], F32, rows_off + K(4))
        d['tmp'] = ar.tile([128, 512], F32, rows_off + K(6))
        d['sq2'] = [ar.tile([128, FT, 512], BF16, cb_off + i * K(8)) for i in range(2)]
        d['xb2'] = ([ar.tile([128, FT, 512], BF16, cb_off + K(16) + i * K(8)) for i in range(2)]
                    if with_xb else [None, None])
        return d

    emit_ln(b, sh, pool, lambda ft: x[:, ft, :], lambda ft, t0, t1: xi[:, ft, t0:t1],
            FT, T, lnscr(K(32), K(64), True), mm, ev)
    barrier(b, ALL, [ev, mm])

    def shift(src_fn, dst):
        for ft in range(4):
            b.op('vector', lambda v, ft=ft: v.tensor_copy(dst[:, ft, :], src_fn(ft)), dr=False)
        for ft in range(4, 8):
            b.op('vector', lambda v, ft=ft: v.tensor_copy(dst[:, ft, 1:T], src_fn(ft)[:, 0:T-1]), dr=False)
            b.op('vector', lambda v, ft=ft: v.memset(dst[:, ft, 0:1], 0.0), dr=False)
        b.op('vector', lambda v: v.engine_nop(), sig=ev)

    import os as _o5
    NL = int(_o5.environ.get('F1NL', '2'))
    NOFFN1 = _o5.environ.get('NOFFN')
    for L in range(NL):
        qkvr = W['wqkvh'][L].rearrange('p (a c) -> p a c', a=8)
        wor = W['woh'][L].rearrange('p (a c) -> p a c', a=4)
        w1r = W['w1h'][L].rearrange('p (a c) -> p a c', a=8)
        w2r = W['w2h'][L].rearrange('p (a c) -> p a c', a=22)
        shift(lambda ft: xi[:, ft, :], hs)
        barrier(b, ALL, [ev, mm])
        wqk = [ar.tile([128, 8, 128], BF16, K(112) + i * 2048, f'{tag}_wqk{L}_{i}') for i in range(2)]
        wfree = [(None, 0)] * 2
        for oc in range(8):
            par = oc % 2
            s0, c0 = wfree[par]
            if s0 is not None:
                b.wait('gpsimd', s0, c0)
            b.dma('gpsimd', wqk[par][:], qkvr[:, :, oc*128:(oc+1)*128], ldw)
            b.wait('tensor', ldw, ldw.n)
            for tc in range(4):
                idx, bank = pool.acquire()
                for kt in range(FT):
                    b.op('tensor', lambda te, kt=kt, bank=bank, par=par, tc=tc: te.matmul(
                        bank[:, 0:512], wqk[par][:, kt, :], hs[:, kt, tc*512:(tc+1)*512],
                        start=(kt == 0), stop=(kt == FT - 1)), sig=(mm if kt == FT - 1 else None))
                b.wait('scalar', mm, mm.n)
                b.op('scalar', lambda s, bank=bank, oc=oc, tc=tc: s.activation(
                    out=qkfm[:, oc, tc*512:(tc+1)*512], in_=bank[:, 0:512], func=AF.Identity), sig=ev)
                pool.set_free(idx, ev, ev.n)
            wfree[par] = (ev, ev.n)
        wv_t = ar.tile([128, 8, 512], BF16, K(116), f'{tag}_wvb{L}')
        b.dma('gpsimd', wv_t[:], qkvr[:, :, 1024:1536], ldw)
        b.wait('tensor', ldw, ldw.n)
        for tc16 in range(16):
            idx, bank = pool.acquire()
            for kt in range(FT):
                b.op('tensor', lambda te, kt=kt, bank=bank, tc16=tc16: te.matmul(
                    bank[:, 0:512], hs[:, kt, tc16*128:(tc16+1)*128], wv_t[:, kt, :],
                    start=(kt == 0), stop=(kt == FT - 1)), sig=(mm if kt == FT - 1 else None))
            b.wait('vector', mm, mm.n)
            b.op('vector', lambda v, bank=bank, tc16=tc16: v.tensor_copy(
                vtm[:, tc16, :], bank[:, 0:512]), sig=ev, dr=False)
            pool.set_free(idx, ev, ev.n)
        barrier(b, ALL, [ev, mm, ldw])

        b.dma('gpsimd', abt[:], W['ab'][:, :].rearrange('p (a c) -> p a c', a=4), ldw)
        barrier(b, ALL, [ldw])
        for hd in range(NH):
            for which in range(2):
                ft_s = which * 4 + (hd * 64) // 128
                pb = (hd * 64) % 128
                ssl = qkfm[pb:pb+64, ft_s, :]
                qcol = 2 * L + which
                acol = 2 * which
                b.op('vector', lambda v, ssl=ssl: v.tensor_copy(q1[:, :], ssl))
                b.op('vector', lambda v, qcol=qcol: v.tensor_scalar(
                    q1[:, :], q1[:, :], qs_t[:, qcol:qcol+1], None, op0=ALU.mult), sig=ev)
                b.wait('tensor', ev, ev.n)
                for tc in range(4):
                    idx, bank = pool.acquire()
                    b.op('tensor', lambda te, bank=bank, tc=tc: te.matmul(
                        bank[0:64, 0:512], rot_t[:, :], q1[:, tc*512:(tc+1)*512],
                        start=True, stop=True), sig=mm)
                    b.wait('scalar', mm, mm.n)
                    b.op('scalar', lambda s, bank=bank, tc=tc: s.activation(
                        out=rots[:, tc*512:(tc+1)*512], in_=bank[0:64, 0:512], func=AF.Identity), sig=ev)
                    pool.set_free(idx, ev, ev.n)
                b.wait('vector', ev, ev.n)
                b.op('vector', lambda v, acol=acol: v.tensor_tensor(
                    q2[:, :], q1[:, :], abt[:, acol, :], op=ALU.mult), dr=False)
                b.op('vector', lambda v, acol=acol: v.tensor_tensor(
                    rots[:, :], rots[:, :], abt[:, acol+1, :], op=ALU.mult))
                b.op('vector', lambda v: v.tensor_tensor(
                    q2[:, :], q2[:, :], rots[:, :], op=ALU.add), sig=ev)
                b.wait('vector', ev, ev.n)
                b.op('vector', lambda v, ssl=ssl: v.tensor_copy(rots[:, :], ssl), sig=ev)
                b.wait('scalar', ev, ev.n)
                b.op('scalar', lambda s: s.activation(
                    out=rots[:, :], in_=rots[:, :], func=AF.Square), sig=ev)
                b.wait('tensor', ev, ev.n)
                for tc in range(4):
                    idx, bank = pool.acquire()
                    b.op('tensor', lambda te, bank=bank, tc=tc: te.matmul(
                        bank[0:1, 0:512], sh.ones_col_bf[0:64, :], rots[:, tc*512:(tc+1)*512],
                        start=True, stop=True), sig=mm)
                    b.wait('scalar', mm, mm.n)
                    b.op('scalar', lambda s, bank=bank: s.activation(
                        out=ssq[0:1, :], in_=bank[0:1, 0:512], func=AF.Sqrt), sig=ev)
                    pool.set_free(idx, ev, ev.n)
                    b.wait('vector', ev, ev.n)
                    b.op('vector', lambda v: v.tensor_scalar(
                        rrow[0:1, :], ssq[0:1, :], 1e-12, None, op0=ALU.max), dr=False)
                    b.op('vector', lambda v: v.reciprocal(rrow[0:1, :], rrow[0:1, :]), sig=ev)
                    b.wait('tensor', ev, ev.n)
                    idx2, bank2 = pool.acquire()
                    b.op('tensor', lambda te, bank2=bank2: te.matmul(
                        bank2[0:64, 0:512], sh.ones_row_f32[0:1, 0:64], rrow[0:1, :],
                        start=True, stop=True), sig=mm)
                    b.wait('vector', mm, mm.n)
                    b.op('vector', lambda v, bank2=bank2, tc=tc: v.tensor_tensor(
                        q2[:, tc*512:(tc+1)*512], q2[:, tc*512:(tc+1)*512], bank2[0:64, 0:512],
                        op=ALU.mult), sig=ev)
                    pool.set_free(idx2, ev, ev.n)
                b.op('vector', lambda v, ssl=ssl: v.tensor_copy(ssl, q2[:, :]), sig=ev)
        barrier(b, ALL, [ev, mm])

        b.dma('gpsimd', maskq[:], W['maskq'][:, :].rearrange('p (a c) -> p a c', a=4), ldw)
        b.dma('gpsimd', maskt[:], W['maskt'][:, :].rearrange('p (a c) -> p a c', a=8), ldw)
        b.dma('gpsimd', wo_t[:], wor, ldw)
        barrier(b, ALL, [ldw])
        for w in range(4):
            kcs = list(range(8)) if w > 0 else list(range(4, 8))
            key0 = (w - 1) * 512
            nj = len(kcs)
            for hd in range(NH):
                ft_q = (hd * 64) // 128
                pb = (hd * 64) % 128
                qsl = qkfm[pb:pb+64, ft_q, :]
                ksl = qkfm[pb:pb+64, 4 + ft_q, :]
                ftc, pbase = hd // 2, (hd % 2) * 64
                for qc in range(4):
                    qoff = w * 512 + qc * 128
                    nk = nj * 128
                    banks = []
                    for half in range(nj // 4):
                        idx, bank = pool.acquire()
                        koff = key0 + kcs[half*4] * 128
                        b.op('tensor', lambda te, bank=bank, qoff=qoff, koff=koff, qsl=qsl, ksl=ksl: te.matmul(
                            bank[:, 0:512], qsl[:, qoff:qoff+128], ksl[:, koff:koff+512],
                            start=True, stop=True), sig=mm)
                        banks.append((idx, bank, half))
                    b.wait('vector', mm, mm.n)
                    for (idx, bank, half) in banks:
                        mq0 = (1024 - nk) + half * 512
                        b.op('vector', lambda v, bank=bank, qc=qc, half=half, mq0=mq0: v.scalar_tensor_tensor(
                            srow[:, half*512:(half+1)*512], bank[:, 0:512], 8.0,
                            maskq[:, qc, mq0:mq0+512], op0=ALU.mult, op1=ALU.add), dr=False)
                    b.op('vector', lambda v: v.engine_nop(), sig=ev)
                    for (idx, bank, half) in banks:
                        pool.set_free(idx, ev, ev.n)
                    b.op('vector', lambda v, qc=qc, nk=nk: v.reduce_max(
                        m_all[:, qc:qc+1], srow[:, 0:nk], axis=AX.X), sig=ev)
                b.op('vector', lambda v: v.tensor_copy(mbf[:, :], m_all[:, :]), sig=ev)
                b.wait('tensor', ev, ev.n)
                b.op('tensor', lambda te: te.transpose(psbf[0:4, 0:128], mbf[:, :], sh.ident_bf[:, :]), sig=mm)
                b.wait('vector', mm, mm.n)
                b.op('vector', lambda v: v.tensor_copy(mt4[0:4, :], psbf[0:4, 0:128]), sig=ev)
                b.wait('tensor', ev, ev.n)
                idxm, bankm = pool.acquire()
                for qc in range(4):
                    b.op('tensor', lambda te, bankm=bankm, qc=qc: te.matmul(
                        bankm[:, qc*128:(qc+1)*128], sh.sel16_bf[0:4, qc, :], mt4[0:4, :],
                        start=True, stop=True), sig=(mm if qc == 3 else None))
                b.wait('vector', mm, mm.n)
                b.op('vector', lambda v, bankm=bankm: v.tensor_copy(mrep[:, :], bankm[:, 0:512]), sig=ev)
                pool.set_free(idxm, ev, ev.n)
                if zfree[0] is not None:
                    b.wait('tensor', zfree[0], zfree[1])
                if cfree[0] is not None:
                    b.wait('tensor', cfree[0], cfree[1])
                for blk in range(nj // 4):
                    b.wait('scalar', mm, mm.n)
                    b.wait('scalar', ev, ev.n)
                    for jj in range(4):
                        j = blk * 4 + jj
                        kc = kcs[j]
                        koff = key0 + kc * 128
                        qoff = w * 512
                        idx, bank = pool.acquire()
                        b.op('tensor', lambda te, bank=bank, koff=koff, qoff=qoff, qsl=qsl, ksl=ksl: te.matmul(
                            bank[:, 0:512], ksl[:, koff:koff+128], qsl[:, qoff:qoff+512],
                            start=True, stop=True), sig=mm)
                        b.wait('vector', mm, mm.n)
                        b.op('vector', lambda v, bank=bank, kc=kc, j=j: v.scalar_tensor_tensor(
                            eT[:, j, :], bank[:, 0:512], 8.0, maskt[:, kc, :],
                            op0=ALU.mult, op1=ALU.add), dr=False)
                        b.op('vector', lambda v, j=j: v.tensor_tensor(
                            eT[:, j, :], eT[:, j, :], mrep[:, :], op=ALU.subtract), sig=ev)
                        pool.set_free(idx, ev, ev.n)
                        b.wait('scalar', ev, ev.n)
                        b.op('scalar', lambda s, j=j: s.activation(
                            out=eT[:, j, :], in_=eT[:, j, :], func=AF.Exp), sig=ev)
                    b.wait('tensor', ev, ev.n)
                    for jj in range(4):
                        j = blk * 4 + jj
                        kc = kcs[j]
                        vt = (w - 1) * 4 + kc
                        b.op('tensor', lambda te, j=j, nj=nj: te.matmul(
                            zps[0:1, 0:512], sh.ones_col_bf[:, :], eT[:, j, :],
                            start=(j == 0), stop=(j == nj - 1)), sig=(mm if jj == 3 else None))
                        b.op('tensor', lambda te, j=j, nj=nj, vt=vt, hd=hd: te.matmul(
                            cps[0:64, 0:512], vtm[:, vt, hd*64:hd*64+64], eT[:, j, :],
                            start=(j == 0), stop=(j == nj - 1)), sig=(mm if jj == 3 else None))
                b.wait('vector', mm, mm.n)
                b.op('vector', lambda v: v.reciprocal(zrowf[0:1, :], zps[0:1, 0:512]), sig=ev)
                zfree = (ev, ev.n)
                b.wait('tensor', ev, ev.n)
                ib, bb = pool.acquire()
                b.op('tensor', lambda te, bb=bb: te.matmul(
                    bb[0:64, 0:512], sh.ones_row_f32[0:1, 0:64], zrowf[0:1, :],
                    start=True, stop=True), sig=mm)
                b.wait('vector', mm, mm.n)
                b.op('vector', lambda v, bb=bb: v.tensor_copy(mrep[:, :], bb[:, 0:512]), sig=ev)
                pool.set_free(ib, ev, ev.n)
                b.op('vector', lambda v, ftc=ftc, pbase=pbase: v.tensor_tensor(
                    ctx4[pbase:pbase+64, ftc, :], cps[0:64, 0:512], mrep[0:64, 0:512],
                    op=ALU.mult), sig=ev)
                cfree = (ev, ev.n)
            b.wait('tensor', ev, ev.n)
            for oc in range(FT):
                idx, bank = pool.acquire()
                for kt in range(4):
                    b.op('tensor', lambda te, kt=kt, bank=bank, oc=oc: te.matmul(
                        bank[:, 0:512], wo_t[:, kt, oc*128:(oc+1)*128], ctx4[:, kt, :],
                        start=(kt == 0), stop=(kt == 3)), sig=(mm if kt == 3 else None))
                b.wait('vector', mm, mm.n)
                b.op('vector', lambda v, bank=bank, oc=oc, w=w: v.tensor_tensor(
                    xi[:, oc, w*512:(w+1)*512], xi[:, oc, w*512:(w+1)*512], bank[:, 0:512],
                    op=ALU.add), sig=ev)
                pool.set_free(idx, ev, ev.n)
            b.wait('vector', mm, mm.n)
        barrier(b, ALL, [ev, mm])
        if NOFFN1:
            continue
        shift(lambda ft: xi[:, ft, :], hs)
        barrier(b, ALL, [ev, mm])
        emit_ln(b, sh, pool, lambda ft: hs[:, ft, :], lambda ft, t0, t1: tff[:, ft, t0:t1],
                FT, T, lnscr(K(64), K(72), False), mm, ev,
                gamma=fv[:, L*18:L*18+8], beta=fv[:, L*18+8:L*18+16], in_bf16=True)
        barrier(b, ALL, [ev, mm])
        w1free = [(None, 0)] * 2
        w2free = [(None, 0)] * 2
        wi = w2i = 0
        for tq in range(4):
            for oc in range(44):
                par = wi % 2
                wi += 1
                s0, c0 = w1free[par]
                if s0 is not None:
                    b.wait('gpsimd', s0, c0)
                b.dma('gpsimd', w1c[par][:], w1r[:, :, oc*128:(oc+1)*128], ldw)
                b.wait('tensor', ldw, ldw.n)
                idx, bank = pool.acquire()
                for kt in range(FT):
                    b.op('tensor', lambda te, kt=kt, bank=bank, par=par, tq=tq: te.matmul(
                        bank[:, 0:512], w1c[par][:, kt, :], tff[:, kt, tq*512:(tq+1)*512],
                        start=(kt == 0), stop=(kt == FT - 1)), sig=(mm if kt == FT - 1 else None))
                b.wait('scalar', mm, mm.n)
                fn = AF.Identity if oc < 22 else AF.Gelu
                b.op('scalar', lambda s, bank=bank, oc=oc, fn=fn, L=L: s.activation(
                    out=h1q[:, oc, :], in_=bank[:, 0:512], func=fn,
                    bias=fv2[:, L*52+oc:L*52+oc+1]), sig=ev)
                pool.set_free(idx, ev, ev.n)
                w1free[par] = (ev, ev.n)
            b.wait('vector', ev, ev.n)
            for j in range(22):
                b.op('vector', lambda v, j=j: v.tensor_tensor(
                    h1q[:, j, :], h1q[:, j, :], h1q[:, 22+j, :], op=ALU.mult),
                    dr=(j == 21), sig=(ev if j == 21 else None))
            for oc2 in range(8):
                par = w2i % 2
                w2i += 1
                s0, c0 = w2free[par]
                if s0 is not None:
                    b.wait('gpsimd', s0, c0)
                b.dma('gpsimd', w2c[par][:], w2r[:, :, oc2*128:(oc2+1)*128], ldw)
                b.wait('tensor', ldw, ldw.n)
                b.wait('tensor', ev, ev.n)
                idx, bank = pool.acquire()
                for kt in range(22):
                    b.op('tensor', lambda te, kt=kt, bank=bank, par=par: te.matmul(
                        bank[:, 0:512], w2c[par][:, kt, :], h1q[:, kt, :],
                        start=(kt == 0), stop=(kt == 21)), sig=(mm if kt == 21 else None))
                w2free[par] = (mm, mm.n)
                b.wait('scalar', mm, mm.n)
                b.wait('scalar', ev, ev.n)
                b.op('scalar', lambda s, bank=bank, oc2=oc2, L=L: s.activation(
                    out=otmp[:, :], in_=bank[:, 0:512], func=AF.Identity,
                    bias=fv2[:, L*52+44+oc2:L*52+44+oc2+1]), sig=ev)
                b.wait('vector', ev, ev.n)
                b.op('vector', lambda v, oc2=oc2, tq=tq: v.tensor_tensor(
                    xi[:, oc2, tq*512:(tq+1)*512], xi[:, oc2, tq*512:(tq+1)*512], otmp[:, :],
                    op=ALU.add), sig=ev)
                pool.set_free(idx, ev, ev.n)
        barrier(b, ALL, [ev, mm])
    for ft in range(FT):
        b.op('vector', lambda v, ft=ft: v.scalar_tensor_tensor(
            x[:, ft, :], xi[:, ft, :], coef, x[:, ft, :], op0=ALU.mult, op1=ALU.add), sig=ev)
    barrier(b, ALL, [ev, mm])
    b.f1_dbg = dict(xi=xi, ctx=ctx4, qkfm=qkfm, vtm=vtm)


def a1_dram(b, pfx):
    d = {}
    d['winh'] = b.din(pfx+'winh', [128, 8*2176], F32)
    d['wouth'] = b.din(pfx+'wouth', [128, 8*1024], F32)
    d['wret'] = b.din(pfx+'wret', [128, 64], F32)
    d['thet'] = b.din(pfx+'thet', [128, 32], F32)
    d['nsv'] = b.din(pfx+'nsv', [128, 2], F32)
    d['wcv'] = b.din(pfx+'wcv', [128, 17*4], F32)
    d['bcv'] = b.din(pfx+'bcv', [128, 17], F32)
    d['ssc'] = b.din(pfx+'ssc', [32, 1], F32)
    d['slp'] = b.din(pfx+'slp', [1, 32], F32)
    d['dl'] = b.din(pfx+'dl', [1, 3], F32)
    d['dpos'] = b.din(pfx+'dpos', [2, 2048], F32)
    d['selg'] = b.din(pfx+'selg', [32, 32, 32], F32)
    return d


def a1_host_prep(I, pfx_in='L1_attn_'):
    g = lambda k: np.asarray(I[pfx_in + k], dtype=np.float32)
    lhsT = lambda w: np.ascontiguousarray(
        w.reshape(w.shape[0] // 128, 128, w.shape[1]).transpose(1, 0, 2).reshape(128, -1))
    out = {}
    win = np.zeros((1024, 2176), np.float32)
    win[:, 0:2080] = g('W_in')
    out['winh'] = lhsT(win)
    out['wouth'] = lhsT(g('W_out'))
    wre = g('W_re'); wim = g('W_im')
    perm = np.zeros(128, np.int64)
    for h in range(32):
        for m in range(4):
            perm[m*32 + h] = h*4 + m
    wret = np.zeros((128, 64), np.float32)
    wret[:, 0:32] = wre[perm]
    wret[:, 32:64] = wim[perm]
    out['wret'] = wret
    th = g('theta')[0, 0]
    thet = np.zeros((128, 32), np.float32)
    for k in range(32):
        for h in range(32):
            for m in range(4):
                thet[(k % 4)*32 + h, (k // 4)*4 + m] = th[k, h, m]
    out['thet'] = thet
    ns = g('norm_scale')
    nsv = np.zeros((128, 2), np.float32)
    nsv[:, 0] = ns[0:128][perm]
    nsv[:, 1] = ns[128:256][perm]
    out['nsv'] = nsv
    wc = g('w_conv')[:, 0, :]
    wcv = np.zeros((128, 68), np.float32)
    bc = g('b_conv')
    bcv = np.zeros((128, 17), np.float32)
    for ct in range(17):
        cs = slice(ct*128, min((ct+1)*128, 2080))
        n = cs.stop - cs.start
        for tau in range(4):
            wcv[0:n, ct*4+tau] = wc[tau, cs]
        bcv[0:n, ct] = bc[cs]
    out['wcv'] = wcv
    out['bcv'] = bcv
    out['ssc'] = np.ascontiguousarray(g('score_scale').reshape(32, 1))
    slp = np.zeros((1, 32), np.float32)
    slp[0, 0:28] = g('decay_slopes'); slp[0, 28:32] = g('anchor_slopes')
    out['slp'] = slp
    out['dl'] = np.ascontiguousarray(g('deriv_logits').reshape(1, 3))
    dpos = np.zeros((2, 2048), np.float32)
    dpos[0] = 2047.0 - np.arange(2048); dpos[1] = np.arange(2048)
    out['dpos'] = dpos
    sel32 = np.zeros((32, 32, 32), np.float32)
    for r in range(32):
        sel32[r, r, :] = 1.0
    out['selg'] = sel32
    return out


def emit_a1(b, sh, pool, W, x, ar, mm, ev, ldw, T=2048, tag='a1'):
    FT = 8
    T2 = T // 2
    PI = 3.14159265358979
    K = lambda n: n * 1024
    ALL = ENGINES
    sh.make_ident_sel()
    sh.const(1.0)
    barrier(b, ALL, [sh.init_sig])

    u = ar.tile([128, FT, T], BF16, K(0), f'{tag}_u')
    xval = ar.tile([128, 8, T2], BF16, K(32), f'{tag}_xval')
    gate = ar.tile([128, 8, T2], BF16, K(48), f'{tag}_gate')
    sraw = ar.tile([32, T2], F32, K(64), f'{tag}_sraw')
    ztile = [ar.tile([128, 1028], BF16, K(68) + i * 2560, f'{tag}_zt{i}') for i in range(2)]
    winc = [ar.tile([128, 8, 128], BF16, K(73) + i * 2048, f'{tag}_winc{i}') for i in range(2)]
    stash = ar.tile([128, 17, 3], BF16, K(77), f'{tag}_stash')
    pw = ar.tile([32, T2], F32, K(78), f'{tag}_pw')
    pwbf = ar.tile([32, T2], BF16, K(82), f'{tag}_pwbf')
    pwrep = ar.tile([128, T2], F32, K(84), f'{tag}_pwrep')
    invden = ar.tile([128, T2], F32, K(88), f'{tag}_invden')
    wA = ar.tile([128, T2], F32, K(92), f'{tag}_wA')
    wB = ar.tile([128, T2], F32, K(96), f'{tag}_wB')
    wC = ar.tile([128, T2], F32, K(100), f'{tag}_wC')
    wD = ar.tile([128, T2], F32, K(104), f'{tag}_wD')
    rek = ar.tile([128, 4, T2], BF16, K(108), f'{tag}_rek')
    imk = ar.tile([128, 4, T2], BF16, K(64), f'{tag}_imk')
    y = ar.tile([128, 8, T2], BF16, K(116), f'{tag}_y')
    woutc = [ar.tile([128, 8, 128], BF16, K(92) + i * 2048, f'{tag}_woc{i}') for i in range(2)]
    wret = b.sb(f'{tag}s_wret', [128, 64], F32)
    wreb = b.sb(f'{tag}s_wreb', [128, 64], BF16)
    thet = b.sb(f'{tag}s_thet', [128, 32], F32)
    nsv = b.sb(f'{tag}s_nsv', [128, 2], F32)
    wcv = b.sb(f'{tag}s_wcv', [128, 68], F32)
    bcv = b.sb(f'{tag}s_bcv', [128, 17], F32)
    ssc = b.sb(f'{tag}s_ssc', [32, 1], F32)
    slp = b.sb(f'{tag}s_slp', [1, 32], F32)
    sprow = b.sb(f'{tag}s_sprow', [1, 32], F32)
    spdec = b.sb(f'{tag}s_spdec', [1, 32], F32)
    spanc = b.sb(f'{tag}s_spanc', [1, 32], F32)
    dlr = b.sb(f'{tag}s_dlr', [1, 3], F32)
    ddist = ar.tile([1, T2], F32, K(124), f'{tag}_ddist')
    dpos2 = ar.tile([1, T2], F32, K(128), f'{tag}_dpos2')
    selgt = b.sb(f'{tag}s_selg', [32, 32, 32], BF16)
    w012 = b.sb(f'{tag}s_w012', [128, 3], F32)
    carry = b.sb(f'{tag}s_carry', [128, 24], F32)
    carry2 = b.sb(f'{tag}s_carry2', [128, 72], F32)
    rsqr = ar.tile([1, T2], F32, K(96), f'{tag}_rsqr')
    rsq_sb = ar.tile([32, 512], F32, K(100), f'{tag}_rsqsb2')
    row3 = b.sb(f'{tag}s_row3', [1, 32], F32)
    zps = b.psum(f'{tag}_zps', [128, 512], F32)

    b.dma('sync', wret[:], W['wret'][:, :], ldw)
    b.dma('sync', thet[:], W['thet'][:, :], ldw)
    b.dma('sync', nsv[:], W['nsv'][:, :], ldw)
    b.dma('sync', wcv[:], W['wcv'][:, :], ldw)
    b.dma('sync', bcv[:], W['bcv'][:, :], ldw)
    b.dma('sync', ssc[:], W['ssc'][:, :], ldw)
    b.dma('sync', slp[:], W['slp'][:, :], ldw)
    b.dma('sync', dlr[:], W['dl'][:, :], ldw)
    b.dma('gpsimd', selgt[:], W['selg'][:, :, :], ldw)
    barrier(b, ALL, [ldw])

    def lnscr(rows_off, cb_off):
        d = {}
        d['mean'] = ar.tile([1, 512], F32, rows_off)
        d['var'] = ar.tile([1, 512], F32, rows_off + K(2))
        d['t2'] = ar.tile([1, 512], F32, rows_off + K(4))
        d['tmp'] = ar.tile([128, 512], F32, rows_off + K(6))
        d['sq2'] = [ar.tile([128, FT, 512], BF16, cb_off + i * K(8)) for i in range(2)]
        d['xb2'] = [ar.tile([128, FT, 512], BF16, cb_off + K(16) + i * K(8)) for i in range(2)]
        return d

    emit_ln(b, sh, pool, lambda ft: x[:, ft, :], lambda ft, t0, t1: u[:, ft, t0:t1],
            FT, T, lnscr(K(32), K(64)), mm, ev)
    barrier(b, ALL, [ev, mm])

    b.op('vector', lambda v: v.scalar_tensor_tensor(
        sprow[0:1, :], slp[0:1, :], -1.0, slp[0:1, :], op0=ALU.mult, op1=ALU.max))
    b.op('vector', lambda v: v.tensor_copy(row3[0:1, :], sprow[0:1, :]), sig=ev)
    b.wait('scalar', ev, ev.n)
    b.op('scalar', lambda s: s.activation(out=row3[0:1, :], in_=row3[0:1, :], func=AF.Exp, scale=-1.0), sig=ev)
    b.wait('scalar', ev, ev.n)
    b.op('scalar', lambda s: s.activation(out=row3[0:1, :], in_=row3[0:1, :], func=AF.Ln,
                                          bias=sh.const(1.0)[0:1, 0:1]), sig=ev)
    b.wait('vector', ev, ev.n)
    b.op('vector', lambda v: v.scalar_tensor_tensor(
        sprow[0:1, :], slp[0:1, :], 0.0, row3[0:1, :], op0=ALU.max, op1=ALU.add))
    b.op('vector', lambda v: v.memset(spdec[0:1, :], 0.0), dr=False)
    b.op('vector', lambda v: v.memset(spanc[0:1, :], 0.0))
    b.op('vector', lambda v: v.tensor_copy(spdec[0:1, 0:28], sprow[0:1, 0:28]), dr=False)
    b.op('vector', lambda v: v.tensor_copy(spanc[0:1, 28:32], sprow[0:1, 28:32]), sig=ev)
    b.op('vector', lambda v: v.reduce_max(row3[0:1, 0:1], dlr[0:1, :], axis=AX.X, negate=True), sig=ev)
    b.wait('scalar', ev, ev.n)
    b.op('scalar', lambda s: s.activation(out=row3[0:1, 0:3], in_=dlr[0:1, :], func=AF.Exp,
                                          bias=row3[0:1, 0:1]), sig=ev)
    b.wait('vector', ev, ev.n)
    b.op('vector', lambda v: v.tensor_scalar(row3[0:1, 0:3], row3[0:1, 0:3], 1.0, 0.0,
                                             op0=ALU.mult, op1=ALU.add, accum_out=row3[0:1, 3:4]))
    b.op('vector', lambda v: v.reciprocal(row3[0:1, 4:5], row3[0:1, 3:4]))
    b.op('vector', lambda v: v.tensor_scalar(row3[0:1, 0:3], row3[0:1, 0:3], row3[0:1, 4:5], None,
                                             op0=ALU.mult), sig=ev)
    b.wait('tensor', ev, ev.n)
    for i3 in range(3):
        idx, bank = pool.acquire()
        b.op('tensor', lambda te, bank=bank, i3=i3: te.matmul(
            bank[:, 0:1], sh.ones_row_f32[0:1, :], row3[0:1, i3:i3+1], start=True, stop=True), sig=mm)
        b.wait('vector', mm, mm.n)
        sc3 = -1.0 if i3 == 2 else 1.0
        b.op('vector', lambda v, bank=bank, i3=i3, sc3=sc3: v.tensor_scalar(
            w012[:, i3:i3+1], bank[:, 0:1], sc3, None, op0=ALU.mult), sig=ev)
        pool.set_free(idx, ev, ev.n)
    b.op('vector', lambda v: v.tensor_scalar(wret[:, 0:32], wret[:, 0:32], nsv[:, 0:1], None, op0=ALU.mult), dr=False)
    b.op('vector', lambda v: v.tensor_scalar(wret[:, 32:64], wret[:, 32:64], nsv[:, 1:2], None, op0=ALU.mult))
    b.op('vector', lambda v: v.tensor_copy(wreb[:], wret[:]), sig=ev)
    barrier(b, ALL, [ev, mm])

    winr = W['winh'][:, :].rearrange('p (a c) -> p a c', a=8)
    woutr = W['wouth'][:, :].rearrange('p (a c) -> p a c', a=8)

    for half in range(2):
        T0 = half * T2
        b.dma('sync', ddist[:], W['dpos'][0:1, T0:T0+T2], ldw)
        b.dma('sync', dpos2[:], W['dpos'][1:2, T0:T0+T2], ldw)
        barrier(b, ['tensor'], [ldw])
        wfree = [(None, 0)] * 2
        zfree_ = [(None, 0)] * 2
        for oc in range(17):
            par = oc % 2
            s0, c0 = wfree[par]
            if s0 is not None:
                b.wait('gpsimd', s0, c0)
            b.dma('gpsimd', winc[par][:], winr[:, :, oc*128:(oc+1)*128], ldw)
            b.wait('tensor', ldw, ldw.n)
            zt = ztile[par]
            s0, c0 = zfree_[par]
            if s0 is not None:
                b.wait('scalar', s0, c0)
            for tc in range(2):
                idx, bank = pool.acquire()
                for kt in range(FT):
                    b.op('tensor', lambda te, kt=kt, bank=bank, par=par, tc=tc, T0=T0: te.matmul(
                        bank[:, 0:512], winc[par][:, kt, :], u[:, kt, T0+tc*512:T0+(tc+1)*512],
                        start=(kt == 0), stop=(kt == FT - 1)), sig=(mm if kt == FT - 1 else None))
                b.wait('scalar', mm, mm.n)
                b.op('scalar', lambda s, bank=bank, zt=zt, tc=tc: s.activation(
                    out=zt[:, 3+tc*512:3+(tc+1)*512], in_=bank[:, 0:512], func=AF.Identity), sig=ev)
                pool.set_free(idx, ev, ev.n)
            wfree[par] = (ev, ev.n)
            b.wait('vector', ev, ev.n)
            if half == 0:
                b.op('vector', lambda v, zt=zt: v.memset(zt[:, 0:3], 0.0), sig=ev)
            else:
                b.op('vector', lambda v, zt=zt, oc=oc: v.tensor_copy(zt[:, 0:3], stash[:, oc, :]), sig=ev)
            for tau in range(4):
                if tau == 0:
                    b.op('vector', lambda v, zt=zt, oc=oc: v.tensor_scalar(
                        wA[:, :], zt[:, 0:T2], wcv[:, oc*4:oc*4+1], None, op0=ALU.mult), dr=False)
                else:
                    b.op('vector', lambda v, zt=zt, oc=oc, tau=tau: v.scalar_tensor_tensor(
                        wA[:, :], zt[:, tau:tau+T2], wcv[:, oc*4+tau:oc*4+tau+1], wA[:, :],
                        op0=ALU.mult, op1=ALU.add), dr=False)
            if oc < 8:
                b.op('vector', lambda v, oc=oc: v.tensor_scalar(
                    xval[:, oc, :], wA[:, :], bcv[:, oc:oc+1], None, op0=ALU.add), sig=ev)
            elif oc < 16:
                b.op('vector', lambda v, oc=oc: v.tensor_scalar(
                    wA[:, :], wA[:, :], bcv[:, oc:oc+1], None, op0=ALU.add), sig=ev)
                b.wait('scalar', ev, ev.n)
                b.op('scalar', lambda s, oc=oc: s.activation(
                    out=gate[:, oc-8, :], in_=wA[:, :], func=AF.Silu), sig=ev)
            else:
                b.op('vector', lambda v, oc=oc: v.tensor_scalar(
                    sraw[:, :], wA[0:32, :], bcv[0:32, oc:oc+1], None, op0=ALU.add), sig=ev)
            if half == 0:
                b.op('vector', lambda v, zt=zt, oc=oc: v.tensor_copy(
                    stash[:, oc, :], zt[:, T2:T2+3]), sig=ev)
            zfree_[par] = (ev, ev.n)
        barrier(b, ALL, [ev, mm])
        idx, bankt = pool.acquire()
        b.op('tensor', lambda te, bankt=bankt: te.matmul(
            bankt[0:32, 0:512], spdec[0:1, :], ddist[0:1, 0:512], start=True, stop=False))
        b.op('tensor', lambda te, bankt=bankt: te.matmul(
            bankt[0:32, 0:512], spanc[0:1, :], dpos2[0:1, 0:512], start=False, stop=True))
        idx2, bankt2 = pool.acquire()
        b.op('tensor', lambda te, bankt2=bankt2: te.matmul(
            bankt2[0:32, 0:512], spdec[0:1, :], ddist[0:1, 512:1024], start=True, stop=False))
        b.op('tensor', lambda te, bankt2=bankt2: te.matmul(
            bankt2[0:32, 0:512], spanc[0:1, :], dpos2[0:1, 512:1024], start=False, stop=True), sig=mm)
        b.wait('scalar', mm, mm.n)
        b.op('scalar', lambda s, bankt=bankt: s.activation(
            out=pw[0:32, 0:512], in_=bankt[0:32, 0:512], func=AF.Exp, scale=-1.0), dr=False)
        b.op('scalar', lambda s, bankt2=bankt2: s.activation(
            out=pw[0:32, 512:1024], in_=bankt2[0:32, 0:512], func=AF.Exp, scale=-1.0), sig=ev)
        pool.set_free(idx, ev, ev.n)
        pool.set_free(idx2, ev, ev.n)
        b.wait('vector', ev, ev.n)
        b.op('vector', lambda v: v.tensor_scalar(
            sraw[:, :], sraw[:, :], ssc[:, 0:1], 20.0, op0=ALU.mult, op1=ALU.min), dr=False)
        b.op('vector', lambda v: v.tensor_scalar(
            sraw[:, :], sraw[:, :], -20.0, None, op0=ALU.max), sig=ev)
        b.wait('scalar', ev, ev.n)
        b.op('scalar', lambda s: s.activation(out=sraw[:, :], in_=sraw[:, :], func=AF.Exp), sig=ev)
        b.wait('vector', ev, ev.n)
        b.op('vector', lambda v: v.tensor_tensor(pw[:, :], pw[:, :], sraw[:, :], op=ALU.mult), dr=False)
        b.op('vector', lambda v: v.tensor_copy(pwbf[:, :], pw[:, :]), sig=ev)
        barrier(b, ALL, [ev, mm])
        import os as _o8
        if _o8.environ.get('A1DBG2') and half == 0:
            _sd2 = b.sem(f'{tag}dbg2')
            d_pw2 = b.dout(f'{tag}_d_pw2', [32, T2], F32)
            d_sp = b.dout(f'{tag}_d_sp', [1, 32], F32)
            d_p2 = b.dout(f'{tag}_d_p2', [32, T2], F32)
            b.dma('gpsimd', d_pw2[:, :], pw[:], _sd2)
            b.dma('gpsimd', d_sp[:, :], sprow[:], _sd2)
            b.dma('gpsimd', d_p2[:, :], sraw[:], _sd2)
            b.wait('gpsimd', _sd2, _sd2.n)
            barrier(b, ALL, [_sd2])

        for gg in range(8):
            b.wait('tensor', ev, ev.n)
            for k2 in range(4):
                r = gg * 4 + k2
                for tc in range(2):
                    idx, bank = pool.acquire()
                    b.op('tensor', lambda te, bank=bank, r=r, tc=tc: te.matmul(
                        bank[0:32, 0:512], selgt[:, r, :],
                        pwbf[:, tc*512:(tc+1)*512], start=True, stop=True), sig=mm)
                    b.wait('vector', mm, mm.n)
                    b.op('vector', lambda v, bank=bank, k2=k2, tc=tc: v.tensor_copy(
                        pwrep[k2*32:(k2+1)*32, tc*512:(tc+1)*512], bank[0:32, 0:512]), sig=ev, dr=False)
                    pool.set_free(idx, ev, ev.n)
            b.op('vector', lambda v: v.engine_nop(), sig=ev)
            ci = gg * 9 + 8
            if half == 0:
                b.op('vector', lambda v, ci=ci: v.memset(carry2[:, ci:ci+1], 0.0))
            b.op('vector', lambda v, ci=ci: v.tensor_tensor_scan(
                invden[:, :], pwrep[:, :], pwrep[:, :], carry2[:, ci:ci+1],
                op0=ALU.add, op1=ALU.bypass))
            b.op('vector', lambda v, ci=ci: v.tensor_copy(carry2[:, ci:ci+1], invden[:, T2-1:T2]))
            b.op('vector', lambda v: v.tensor_scalar(invden[:, :], invden[:, :], 1e-4, None, op0=ALU.max))
            b.op('vector', lambda v: v.reciprocal(invden[:, :], invden[:, :]), sig=ev)
            for mi in range(4):
                tcol = gg * 4 + mi
                b.wait('vector', ldw, ldw.n)
                b.op('vector', lambda v, gg=gg, tcol=tcol: v.tensor_scalar(
                    wA[:, :], xval[:, gg, :], thet[:, tcol:tcol+1], None, op0=ALU.mult))
                for _rr in range(3):
                    b.op('vector', lambda v: v.tensor_scalar(
                        wB[:, :], wA[:, :], PI, 2*PI, op0=ALU.is_gt, op1=ALU.mult), dr=False)
                    b.op('vector', lambda v: v.tensor_tensor(wA[:, :], wA[:, :], wB[:, :], op=ALU.subtract), dr=False)
                    b.op('vector', lambda v: v.tensor_scalar(
                        wB[:, :], wA[:, :], -PI, 2*PI, op0=ALU.is_lt, op1=ALU.mult), dr=False)
                    b.op('vector', lambda v: v.tensor_tensor(wA[:, :], wA[:, :], wB[:, :], op=ALU.add), dr=False)
                b.op('vector', lambda v: v.engine_nop(), sig=ev)
                b.wait('scalar', ev, ev.n)
                b.op('scalar', lambda s: s.activation(out=wB[:, :], in_=wA[:, :], func=AF.Sin), sig=ev)
                b.op('vector', lambda v: v.tensor_scalar(
                    wC[:, :], wA[:, :], -1.0, PI/2, op0=ALU.mult, op1=ALU.add), dr=False)
                b.op('vector', lambda v: v.tensor_scalar(
                    wD[:, :], wC[:, :], PI, 2*PI, op0=ALU.is_gt, op1=ALU.mult))
                b.op('vector', lambda v: v.tensor_tensor(wC[:, :], wC[:, :], wD[:, :], op=ALU.subtract), sig=ev)
                b.wait('scalar', ev, ev.n)
                b.op('scalar', lambda s: s.activation(out=wC[:, :], in_=wC[:, :], func=AF.Sin), sig=ev)
                b.wait('vector', ev, ev.n)
                b.op('vector', lambda v, gg=gg: v.tensor_scalar(
                    wD[:, :], xval[:, gg, :], w012[:, 1:2], None, op0=ALU.mult), dr=False)
                b.op('vector', lambda v: v.tensor_scalar(
                    wD[:, :], wD[:, :], w012[:, 0:1], None, op0=ALU.add))
                b.op('vector', lambda v, gg=gg: v.tensor_copy(wA[:, :], xval[:, gg, :]))
                b.op('vector', lambda v: v.tensor_tensor(wA[:, :], wA[:, :], wA[:, :], op=ALU.mult))
                b.op('vector', lambda v: v.scalar_tensor_tensor(
                    wD[:, :], wA[:, :], w012[:, 2:3], wD[:, :], op0=ALU.mult, op1=ALU.add))
                b.op('vector', lambda v: v.tensor_tensor(wC[:, :], wC[:, :], wD[:, :], op=ALU.mult), dr=False)
                b.op('vector', lambda v: v.tensor_tensor(wC[:, :], wC[:, :], pwrep[:, :], op=ALU.mult))
                cri = gg * 9 + mi * 2
                if half == 0:
                    b.op('vector', lambda v, cri=cri: v.memset(carry2[:, cri:cri+1], 0.0))
                b.op('vector', lambda v, cri=cri: v.tensor_tensor_scan(
                    wA[:, :], wC[:, :], wC[:, :], carry2[:, cri:cri+1], op0=ALU.add, op1=ALU.bypass))
                b.op('vector', lambda v, cri=cri: v.tensor_copy(carry2[:, cri:cri+1], wA[:, T2-1:T2]))
                b.op('vector', lambda v: v.tensor_tensor(wA[:, :], wA[:, :], invden[:, :], op=ALU.mult), sig=ev)
                b.op('vector', lambda v: v.tensor_tensor(wB[:, :], wB[:, :], wD[:, :], op=ALU.mult), dr=False)
                b.op('vector', lambda v: v.tensor_tensor(wB[:, :], wB[:, :], pwrep[:, :], op=ALU.mult))
                if half == 0:
                    b.op('vector', lambda v, cri=cri: v.memset(carry2[:, cri+1:cri+2], 0.0))
                b.op('vector', lambda v, cri=cri: v.tensor_tensor_scan(
                    wC[:, :], wB[:, :], wB[:, :], carry2[:, cri+1:cri+2], op0=ALU.add, op1=ALU.bypass))
                b.op('vector', lambda v, cri=cri: v.tensor_copy(carry2[:, cri+1:cri+2], wC[:, T2-1:T2]))
                b.op('vector', lambda v: v.tensor_tensor(wC[:, :], wC[:, :], invden[:, :], op=ALU.mult), sig=ev)
                b.wait('gpsimd', ev, ev.n)
                b.wait('gpsimd', mm, mm.n)
                for k2 in range(4):
                    b.dma('gpsimd', rek[mi*32:(mi+1)*32, k2, :], wA[k2*32:(k2+1)*32, :], ldw)
                    b.dma('gpsimd', imk[mi*32:(mi+1)*32, k2, :], wC[k2*32:(k2+1)*32, :], ldw)
            barrier(b, ['tensor', 'scalar', 'vector'], [ldw, ev])
            for k2 in range(4):
                kk = gg * 4 + k2
                b.wait('scalar', ev, ev.n)
                b.op('scalar', lambda s, k2=k2: s.activation(out=wA[:, :], in_=rek[:, k2, :], func=AF.Square), dr=False)
                b.op('scalar', lambda s, k2=k2: s.activation(out=wB[:, :], in_=imk[:, k2, :], func=AF.Square), sig=ev)
                b.wait('tensor', ev, ev.n)
                for tc in range(2):
                    idx, bank = pool.acquire()
                    b.op('tensor', lambda te, bank=bank, tc=tc: te.matmul(
                        bank[0:1, 0:512], sh.ones_col_f32[:, :], wA[:, tc*512:(tc+1)*512],
                        start=True, stop=False))
                    b.op('tensor', lambda te, bank=bank, tc=tc: te.matmul(
                        bank[0:1, 0:512], sh.ones_col_f32[:, :], wB[:, tc*512:(tc+1)*512],
                        start=False, stop=True), sig=mm)
                    b.wait('scalar', mm, mm.n)
                    b.op('scalar', lambda s, bank=bank, tc=tc: s.activation(
                        out=rsqr[0:1, tc*512:(tc+1)*512], in_=bank[0:1, 0:512], func=AF.Sqrt,
                        scale=1.0/256.0, bias=sh.const(1e-5)[0:1, 0:1]), sig=ev)
                    pool.set_free(idx, ev, ev.n)
                b.wait('vector', ev, ev.n)
                b.op('vector', lambda v: v.reciprocal(rsqr[0:1, :], rsqr[0:1, :]), sig=ev)
                b.wait('tensor', ev, ev.n)
                for tc in range(2):
                    idx, bank = pool.acquire()
                    b.op('tensor', lambda te, bank=bank, k2=k2, tc=tc: te.matmul(
                        bank[0:32, 0:512], wreb[:, 0:32], rek[:, k2, tc*512:(tc+1)*512],
                        start=True, stop=False))
                    b.op('tensor', lambda te, bank=bank, k2=k2, tc=tc: te.matmul(
                        bank[0:32, 0:512], wreb[:, 32:64], imk[:, k2, tc*512:(tc+1)*512],
                        start=False, stop=True), sig=mm)
                    idxr, bankr = pool.acquire()
                    b.op('tensor', lambda te, bankr=bankr, tc=tc: te.matmul(
                        bankr[0:32, 0:512], sh.ones_row_f32[0:1, 0:32], rsqr[0:1, tc*512:(tc+1)*512],
                        start=True, stop=True), sig=mm)
                    b.wait('vector', mm, mm.n)
                    b.op('vector', lambda v, bankr=bankr: v.tensor_copy(rsq_sb[:, :], bankr[0:32, 0:512]), sig=ev)
                    pool.set_free(idxr, ev, ev.n)
                    pb = (kk % 4) * 32
                    fkt = kk // 4
                    b.op('vector', lambda v, bank=bank, pb=pb: v.tensor_tensor(
                        wA[pb:pb+32, 0:512], bank[0:32, 0:512], rsq_sb[:, :], op=ALU.mult), dr=False)
                    b.op('vector', lambda v, pb=pb, fkt=fkt, tc=tc: v.tensor_tensor(
                        y[pb:pb+32, fkt, tc*512:(tc+1)*512], wA[pb:pb+32, 0:512],
                        gate[pb:pb+32, fkt, tc*512:(tc+1)*512], op=ALU.mult), sig=ev)
                    pool.set_free(idx, ev, ev.n)
        barrier(b, ALL, [ev, mm])
        if half == 0:
            b.a1_dbg = dict(xval=xval, gate=gate, sraw=sraw, pw=pw, invden=invden, rek=rek, imk=imk, y=y, u=u)
            import os as _o7
            if _o7.environ.get('A1DBG'):
                _sd = b.sem(f'{tag}dbg')
                douts = {}
                douts['xval'] = b.dout(f'{tag}_d_xval', [1024, T2], F32)
                douts['gate'] = b.dout(f'{tag}_d_gate', [1024, T2], F32)
                douts['sraw'] = b.dout(f'{tag}_d_sraw', [32, T2], F32)
                douts['pw'] = b.dout(f'{tag}_d_pw', [32, T2], F32)
                douts['invden'] = b.dout(f'{tag}_d_invden', [128, T2], F32)
                douts['rek'] = b.dout(f'{tag}_d_rek', [128, 4, T2], F32)
                douts['wA'] = b.dout(f'{tag}_d_wA', [128, T2], F32)
                douts['wB'] = b.dout(f'{tag}_d_wB', [128, T2], F32)
                douts['wC'] = b.dout(f'{tag}_d_wC', [128, T2], F32)
                douts['wD'] = b.dout(f'{tag}_d_wD', [128, T2], F32)
                b.dma('gpsimd', douts['wA'][:, :], wA[:], _sd)
                b.dma('gpsimd', douts['wB'][:, :], wB[:], _sd)
                b.dma('gpsimd', douts['wC'][:, :], wC[:], _sd)
                b.dma('gpsimd', douts['wD'][:, :], wD[:], _sd)
                douts['y'] = b.dout(f'{tag}_d_y', [1024, T2], F32)
                for ft in range(8):
                    b.dma('gpsimd', douts['xval'][ft*128:(ft+1)*128, :], xval[:, ft, :], _sd)
                    b.dma('gpsimd', douts['gate'][ft*128:(ft+1)*128, :], gate[:, ft, :], _sd)
                    b.dma('gpsimd', douts['y'][ft*128:(ft+1)*128, :], y[:, ft, :], _sd)
                b.dma('gpsimd', douts['sraw'][:, :], sraw[:], _sd)
                b.dma('gpsimd', douts['pw'][:, :], pw[:], _sd)
                b.dma('gpsimd', douts['invden'][:, :], invden[:], _sd)
                b.dma('gpsimd', douts['rek'][:, :, :], rek[:], _sd)
                b.wait('gpsimd', _sd, _sd.n)
                barrier(b, ALL, [_sd])
        wofree = [(None, 0)] * 2
        for oc in range(8):
            par = oc % 2
            s0, c0 = wofree[par]
            if s0 is not None:
                b.wait('gpsimd', s0, c0)
            b.dma('gpsimd', woutc[par][:], woutr[:, :, oc*128:(oc+1)*128], ldw)
            b.wait('tensor', ldw, ldw.n)
            for tc in range(2):
                idx, bank = pool.acquire()
                for kt in range(FT):
                    b.op('tensor', lambda te, kt=kt, bank=bank, par=par, tc=tc: te.matmul(
                        bank[:, 0:512], woutc[par][:, kt, :], y[:, kt, tc*512:(tc+1)*512],
                        start=(kt == 0), stop=(kt == FT - 1)), sig=(mm if kt == FT - 1 else None))
                b.wait('vector', mm, mm.n)
                b.op('vector', lambda v, bank=bank, oc=oc, tc=tc, T0=T0: v.tensor_tensor(
                    x[:, oc, T0+tc*512:T0+(tc+1)*512], x[:, oc, T0+tc*512:T0+(tc+1)*512],
                    bank[:, 0:512], op=ALU.add), sig=ev)
                pool.set_free(idx, ev, ev.n)
            wofree[par] = (ev, ev.n)
        barrier(b, ALL, [ev, mm])



def _build_prog(kind):
    T, FT = 2048, 8
    b = mkbuilder()
    x_d = b.din('x_in', [1024, T], F32)
    y_d = b.dout('x_out', [1024, T], F32)
    dram = {'f0': f0_dram, 'a0': a0_dram, 'f1': f1_dram, 'a1': a1_dram}[kind]
    W = dram(b, kind + '_')
    pool = PsumPool(b, n=5)
    ld = b.sem('ld'); mm = b.sem('mm'); ev = b.sem('ev')
    ldw = b.sem('ldw'); st = b.sem('st'); ini = b.sem('ini')
    sh = Shared(b, ini)
    x = b.sb('x_sb', [128, FT, T], F32)
    ar = UnionArena(b, 'ar', (126 if kind == 'a0' else 132) * 1024)
    for ft in range(FT):
        b.dma('sync', x[:, ft, :], x_d[ft*128:(ft+1)*128, :], ld)
    barrier(b, ENGINES, [ld])
    if kind == 'f0':
        emit_f0(b, sh, pool, W, x, ar, mm, ev, ldw, T=T, coef=0.5, tag='f0')
    elif kind == 'a0':
        emit_a0(b, sh, pool, W, x, ar, mm, ev, ldw, T=T, tag='a0')
    elif kind == 'f1':
        emit_f1(b, sh, pool, W, x, ar, mm, ev, ldw, T=T, coef=0.5, tag='f1')
    else:
        emit_a1(b, sh, pool, W, x, ar, mm, ev, ldw, T=T, tag='a1')
    for ft in range(FT):
        b.dma('sync', y_d[ft*128:(ft+1)*128, :], x[:, ft, :], st)
    b.wait('sync', st, st.n)
    b.finalize()
    return b


def kernel(**inputs):
    I = {k: np.asarray(v) for k, v in inputs.items()}
    NCORES = 8
    H = {'f0': f0_host_prep(I), 'a0': a0_host_prep(I),
         'f1': f1_host_prep(I), 'a1': a1_host_prep(I)}
    progs = {k: _build_prog(k) for k in ('f0', 'a0', 'f1', 'a1')}
    xcur = [np.ascontiguousarray(I['L0_attn_x'][c].T) for c in range(NCORES)]
    for kind in ('f0', 'a0', 'f0', 'f1', 'a1', 'f1'):
        hp = H[kind]
        in_maps = [dict({'x_in': xcur[c]},
                        **{kind + '_' + k: hp[k] for k in hp})
                   for c in range(NCORES)]
        res = progs[kind].run(in_maps, core_ids=list(range(NCORES)))
        xcur = [np.ascontiguousarray(np.asarray(res.results[c]['x_out']))
                for c in range(NCORES)]
    out = np.stack([xc.T for xc in xcur], 0)
    return np.ascontiguousarray(out).astype(np.float32, copy=False)
